# version 06
# speedup vs baseline: 1.2959x; 1.0114x over previous
import os
import numpy as np
import ml_dtypes
import concourse.bass as bass
import concourse.mybir as mybir
from concourse.bass_utils import run_bass_kernel_spmd

F32 = mybir.dt.float32
BF16 = mybir.dt.bfloat16
AF = mybir.ActivationFunctionType
ALU = mybir.AluOpType

NCORE = 8
T = 2048
D = 1024
NF = 8
NH = 16
DH = 64
NKV = 4
FFN = 4096
NFC = 32
SEL = 512
NTT = T // 128
ENGS = ("tensor", "vector", "scalar", "gpsimd", "sync")
EPS_LN = 1e-5
EPS_RMS = 1e-6

TAP = os.environ.get("MK_TAP", "")


def _wimg(w):
    Kin, Mout = w.shape
    KC, MC = Kin // 128, Mout // 128
    return np.ascontiguousarray(
        w.reshape(KC, 128, MC, 128).transpose(1, 2, 0, 3).reshape(128, MC * KC * 128)
    ).astype(np.float32)


def _col128(v):
    return np.ascontiguousarray(np.asarray(v).reshape(-1, 128).T).astype(np.float32)


def host_prepare(d):
    w = {}
    w["aWq"] = _wimg(np.asarray(d["L0_attn_a_Wq"]).T)
    w["aWk"] = _wimg(np.asarray(d["L0_attn_a_Wk"]).T)
    w["aWv"] = _wimg(np.asarray(d["L0_attn_a_Wv"]).T)
    w["aWo"] = _wimg(np.asarray(d["L0_attn_a_Wo"]).T)
    w["bW1"] = _wimg(np.asarray(d["L0_attn_b_W1"]).T)
    w["bW2"] = _wimg(np.asarray(d["L0_attn_b_W2"]).T)
    w["bWg"] = _wimg(np.asarray(d["L0_attn_b_Wg"]).T)
    w["bWo"] = _wimg(np.asarray(d["L0_attn_b_Wo"]).T)
    w["bb1"] = _col128(d["L0_attn_b_b1"])
    w["bb2"] = _col128(d["L0_attn_b_b2"])
    w["bbg"] = _col128(d["L0_attn_b_bg"])
    w["bbo"] = _col128(d["L0_attn_b_bo"])
    for nm, key in (("fWq", "L0_ffn_wq1"), ("fWk", "L0_ffn_wk1"), ("fWv", "L0_ffn_wv1"),
                    ("fWo", "L0_ffn_wo1"), ("fWg", "L0_ffn_wg1"), ("fWu", "L0_ffn_wu1"),
                    ("fWd", "L0_ffn_wd1"), ("fWq2", "L0_ffn_wq2"), ("fWk2", "L0_ffn_wk2"),
                    ("fWv2", "L0_ffn_wv2"), ("fWo2", "L0_ffn_wo2"), ("fWg2", "L0_ffn_wg2"),
                    ("fWu2", "L0_ffn_wu2"), ("fWd2", "L0_ffn_wd2"), ("fPw", "L0_ffn_prior_w")):
        w[nm] = _wimg(np.asarray(d[key]))
    for nm, key in (("fG1", "L0_ffn_ln1_1"), ("fG2", "L0_ffn_ln2_1"),
                    ("fG12", "L0_ffn_ln1_2"), ("fG22", "L0_ffn_ln2_2"),
                    ("fGp", "L0_ffn_prior_norm_w")):
        w[nm] = _col128(d[key])
    return w


def host_constants():
    c = {}
    c["ident_bf"] = np.eye(128, dtype=ml_dtypes.bfloat16)
    c["ident_f32"] = np.eye(128, dtype=np.float32)
    c["ones_row"] = np.ones((1, 128), dtype=np.float32)
    c["onesb_col"] = np.ones((128, 1), dtype=ml_dtypes.bfloat16)
    pc = np.zeros((128, 128), dtype=np.float32)
    pc[0, 0:64] = 1.0
    pc[64, 64:128] = 1.0
    c["paircol"] = pc
    inv = 10000.0 ** (-np.arange(32, dtype=np.float64) / 32.0)
    ang = np.arange(T, dtype=np.float64)[None, :] * inv[:, None]
    c["cos_rep"] = np.tile(np.cos(ang), (4, 1)).astype(ml_dtypes.bfloat16)
    c["sin_rep"] = np.tile(np.sin(ang), (4, 1)).astype(ml_dtypes.bfloat16)
    angT = ang.T
    ctm = np.cos(angT).astype(ml_dtypes.bfloat16).reshape(NTT, 128, 32)
    stm = np.sin(angT).astype(ml_dtypes.bfloat16).reshape(NTT, 128, 32)
    c["cos_tm"] = np.ascontiguousarray(ctm.transpose(1, 0, 2).reshape(128, NTT * 32))
    c["sin_tm"] = np.ascontiguousarray(stm.transpose(1, 0, 2).reshape(128, NTT * 32))
    ut = np.triu(np.ones((128, 128), dtype=np.float32), 1).astype(ml_dtypes.bfloat16)
    c["ustrict"] = ut
    rc = np.zeros((64, 128), dtype=ml_dtypes.bfloat16)
    rs = np.zeros((64, 128), dtype=ml_dtypes.bfloat16)
    for p in range(128):
        rc[p % 32, p] = 1.0
        rs[32 + p % 32, p] = 1.0
    c["repmat_c"] = rc
    c["repmat_s"] = rs
    rt = np.zeros((128, 128), dtype=ml_dtypes.bfloat16)
    for B in (0, 64):
        for j in range(32):
            rt[B + 32 + j, B + j] = -1.0
            rt[B + j, B + 32 + j] = 1.0
    c["rotm"] = rt
    return c


class Sem:
    __slots__ = ("h", "v", "name")

    def __init__(self, h, name):
        self.h = h
        self.v = 0
        self.name = name


class Prog:
    def __init__(self, nc):
        self.nc = nc
        self.ops = {e: [] for e in ENGS}
        self.esem = {}
        self.dsem = {}
        self._seen = {e: {} for e in ENGS}
        self.cur_scope = ""

    def scope(self, name):
        self.cur_scope = name

    NQSLOT = 4

    def add_sems(self, stk):
        for e in ENGS:
            self.esem[e] = Sem(stk.enter_context(self.nc.semaphore(f"c_{e}")), f"c_{e}")
        for nm in ("dW0", "dW1", "dW2", "dW3", "dX", "dY", "dZ"):
            self.dsem[nm] = [Sem(stk.enter_context(self.nc.semaphore(f"{nm}_{i}")), f"{nm}_{i}")
                             for i in range(self.NQSLOT)]
        self._qrot = {nm: 0 for nm in self.dsem}

    def _push(self, eng, fn, waits):
        sem = self.esem[eng]
        if eng != "tensor" and sem.v > 0:
            waits = [(sem, sem.v)] + list(waits)
        sem.v += 1
        wl = []
        seen = self._seen[eng]
        for s, v in waits:
            if s is sem and eng == "tensor":
                continue
            if v > 0 and seen.get(s.name, -1) < v:
                wl.append((s, v))
                seen[s.name] = v
        self.ops[eng].append((tuple(wl), fn, sem, self.cur_scope))
        return sem.v

    def op(self, eng, fn, waits=()):
        return self._push(eng, fn, waits)

    def dma(self, eng, q, fn, waits=()):
        slot = self._qrot[q]
        self._qrot[q] = (slot + 1) % self.NQSLOT
        ds = self.dsem[q][slot]
        prev = ds.v
        ds.v += 16

        def wrap(e, _ds=ds, _f=fn):
            _f(e).then_inc(_ds.h, 16)
            return None
        w = list(waits)
        if prev:
            w.append((ds, prev))
        self._push(eng, wrap, w)
        return (ds, ds.v)

    def clock(self, eng):
        return self.esem[eng].v

    def w(self, eng, v):
        return (self.esem[eng], v)

    def dw(self, q, tok):
        assert isinstance(tok, tuple), "dma() now returns wait tokens"
        return tok

    def qwait(self, q):
        return [(s, s.v) for s in self.dsem[q] if s.v > 0]

    def barrier(self, engines=ENGS, dma=None):
        if dma is None:
            dma = len(engines) == len(ENGS)
        snap = {e: self.esem[e].v for e in engines}
        dwaits = []
        if dma:
            for q in self.dsem:
                dwaits.extend(self.qwait(q))
        for e in engines:
            waits = [(self.esem[o], snap[o]) for o in engines if o != e] + dwaits
            self._push(e, lambda eng: eng.nop(), waits)

    def emit(self, block):
        prog = self

        def make(e):
            def run(engobj):
                cur, sid = None, None
                for waits, fn, sem, scope in prog.ops[e]:
                    if scope != cur:
                        if cur:
                            prog.nc.leave_named_scope(cur, sid, False)
                        sid = prog.nc.enter_named_scope(scope, False)[0] if scope else None
                        cur = scope
                    for s, v in waits:
                        engobj.wait_ge(s.h, v)
                    instr = fn(engobj)
                    if instr is None:
                        instr = engobj.nop()
                    instr.then_inc(sem.h, 1)
                if cur:
                    prog.nc.leave_named_scope(cur, sid, False)
            return run
        block.tensor(make("tensor"))
        block.vector(make("vector"))
        block.scalar(make("scalar"))
        block.gpsimd(make("gpsimd"))
        block.sync(make("sync"))


class MK:
    def __init__(self):
        import contextlib
        self.stk = contextlib.ExitStack()
        nc = bass.Bass("TRN2", target_bir_lowering=False)
        self.nc = nc
        self.p = Prog(nc)

        self.xT = nc.dram_tensor("xT", [D, T], F32, kind="ExternalInput")
        self.outT = nc.dram_tensor("outT", [D, T], F32, kind="ExternalOutput")
        self.W = {}
        wshapes = {
            "aWq": 8 * 8, "aWk": 2 * 8, "aWv": 2 * 8, "aWo": 8 * 8,
            "bW1": 8 * 8, "bW2": 8 * 8, "bWg": 24 * 8, "bWo": 8 * 8,
            "fWq": 8 * 8, "fWk": 8 * 8, "fWv": 8 * 8, "fWo": 8 * 8,
            "fWg": 32 * 8, "fWu": 32 * 8, "fWd": 8 * 32, "fPw": 16 * 8,
            "fWq2": 8 * 8, "fWk2": 8 * 8, "fWv2": 8 * 8, "fWo2": 8 * 8,
            "fWg2": 32 * 8, "fWu2": 32 * 8, "fWd2": 8 * 32,
        }
        for nm, nt in wshapes.items():
            self.W[nm] = nc.dram_tensor(nm, [128, nt * 128], F32, kind="ExternalInput")
        for nm, cols in (("bb1", 8), ("bb2", 8), ("bbg", 24), ("bbo", 8),
                         ("fG1", 8), ("fG2", 8), ("fG12", 8), ("fG22", 8), ("fGp", 8)):
            self.W[nm] = nc.dram_tensor(nm, [128, cols], F32, kind="ExternalInput")

        self.spill_x = nc.dram_tensor("spill_x", [D, T], F32, kind="Internal")
        self.spill_h = nc.dram_tensor("spill_h", [D, T], F32, kind="Internal")
        self.spill_mu = nc.dram_tensor("spill_mu", [D, T], BF16, kind="Internal")
        self.spill_lv = nc.dram_tensor("spill_lv", [D, T], BF16, kind="Internal")
        self.rt_scr = nc.dram_tensor("rt_scr", [1, 2048], F32, kind="Internal")
        self.cc_in = {c: nc.dram_tensor(f"cc_in{c}", [1, 2], F32, kind="Internal")
                      for c in (1, 2)}
        self.cc_out = {c: nc.dram_tensor(f"cc_out{c}", [1, 2], F32, kind="Internal",
                                         addr_space="Shared") for c in (1, 2)}

        self.CT = {k: nc.inline_tensor(v, name="c_" + k) for k, v in host_constants().items()}

        sb = self.stk.enter_context
        t = nc.sbuf_tensor
        self.ident = sb(t("ident", [128, 128], BF16))
        self.ones_r = sb(t("ones_r", [1, 128], F32))
        self.onesb_c = sb(t("onesb_c", [128, 1], BF16))
        self.paircol = sb(t("paircol", [128, 128], F32))
        self.iota1p = sb(t("iota1p", [128, 1], F32))
        self.repm_c = sb(t("repm_c", [64, 128], BF16))
        self.repm_s = sb(t("repm_s", [64, 128], BF16))
        self.rotm = sb(t("rotm", [128, 128], BF16))
        self.masks = sb(t("masks", [128, 8 * 512], BF16))
        self.A = sb(t("Abuf", [128, NF * T], F32))
        self.ps = sb(nc.psum_tensor("ps", [128, 6 * 512], F32))
        self.psb = sb(nc.psum_tensor("psb", [128, 2 * 1024], BF16))
        self.st1 = sb(t("st1", [1, T], F32))
        self.st2 = sb(t("st2", [1, T], F32))
        self.gains = sb(t("gains", [128, 5 * 8], F32))
        self.biases = sb(t("biases", [128, 48], F32))
        self.p.add_sems(self.stk)
        self._tb_slot = 0
        self._uid = 0
        self._tb_ev = {0: 0, 1: 0}
        self._tb_cur = 0

    def sbt(self, name, shape, dtype):
        self._uid += 1
        return self.nc.sbuf_tensor(f"{name}_u{self._uid}", shape, dtype)

    def fm(self, buf, c, t0=0, t1=None, TT_=T):
        t1 = TT_ if t1 is None else t1
        return buf[:, c * TT_ + t0: c * TT_ + t1]

    def bank(self, i, p0=0, p1=128, n0=0, n1=512):
        return self.ps[p0:p1, i * 512 + n0: i * 512 + n1]

    def tslot(self):
        b = self._tb_slot
        self._tb_slot = 1 - b
        w = [self.p.w("vector", self._tb_ev[b])] if self._tb_ev[b] else []
        self._tb_cur = b
        return self.psb[:, b * 1024:b * 1024 + 128], w

    def tslot_done(self, evac_clock):
        self._tb_ev[self._tb_cur] = evac_clock

    def load_w_split(self, nm, dst, c0, c1, waits=(), nq=4):
        p = self.p
        per = (c1 - c0) // nq
        return [p.dma("gpsimd", f"dW{i}", lambda e, _i=i, _p=per: e.dma_start(
            out=dst[:, _i * _p * 128:(_i + 1) * _p * 128],
            in_=self.W[nm][:, (c0 + _i * _p) * 128:(c0 + (_i + 1) * _p) * 128]),
            waits=waits) for i in range(nq)]

    def load_w(self, nm, dst, q="dW0", waits=(), c0=0, c1=None, d0=0):
        p = self.p
        c1 = self.W[nm].shape[1] // 128 if c1 is None else c1
        return p.dma("gpsimd", q, lambda e: e.dma_start(
            out=dst[:, d0:d0 + (c1 - c0) * 128], in_=self.W[nm][:, c0 * 128:c1 * 128]),
            waits=waits)

    def gen_masks(self, kind):
        p = self.p
        specs = []
        if kind == "a":
            for i, dd in enumerate((0, 128, 256, 384)):
                specs.append((i, -dd, -1, 1))
            for i, dd in enumerate((-1024, -896, -768, -640)):
                specs.append((4 + i, dd + 1024 - 1, 1, -1))
        elif kind == "dec":
            for i, dd in enumerate((0, 128, 256, 384)):
                specs.append((i, -dd, -1, 1))
        else:
            for i, dd in enumerate((0, 128, 256, 384, 512)):
                specs.append((i, -dd + 63, -1, 1))
        w = [p.w("vector", p.clock("vector")), p.w("tensor", p.clock("tensor")),
             p.w("scalar", p.clock("scalar"))]
        with self.sbt("mk_ones", [128, 512], BF16) as onesb:
            p.op("gpsimd", lambda e: e.memset(onesb[:, :], 1.0), waits=w)
            for slot, base, cm, fstep in specs:
                p.op("gpsimd", lambda e, _s=slot, _b=base, _m=cm, _f=fstep: e.affine_select(
                    self.masks[:, _s * 512:(_s + 1) * 512], onesb[:, :],
                    [[_f, 512]], ALU.is_ge, 0.0, base=_b, channel_multiplier=_m))
            self.mask_ready = p.w("gpsimd", p.clock("gpsimd"))
            for e in ("vector", "scalar", "tensor"):
                p.op(e, lambda eng: eng.nop(), waits=[self.mask_ready])

    def stage_init(self):
        p = self.p
        p.scope("init")
        for nm, dst in (("ident_bf", self.ident), ("ones_row", self.ones_r),
                        ("onesb_col", self.onesb_c), ("paircol", self.paircol),
                        ("repmat_c", self.repm_c), ("repmat_s", self.repm_s),
                        ("rotm", self.rotm)):
            p.dma("sync", "dY", lambda e, _d=dst, _s=self.CT[nm]: e.dma_start(out=_d[:, :], in_=_s[:, :]))
        for i, nm in enumerate(("fG1", "fG2", "fG12", "fG22", "fGp")):
            p.dma("sync", "dY", lambda e, _i=i, _n=nm: e.dma_start(
                out=self.gains[:, _i * 8:(_i + 1) * 8], in_=self.W[_n][:, :]))
        offs = 0
        for nm, ncols in (("bb1", 8), ("bb2", 8), ("bbg", 24), ("bbo", 8)):
            p.dma("sync", "dY", lambda e, _o=offs, _n=nm, _c=ncols: e.dma_start(
                out=self.biases[:, _o:_o + _c], in_=self.W[_n][:, :]))
            offs += ncols
        for c in range(NF):
            p.dma("sync", "dY", lambda e, _c=c: e.dma_start(
                out=self.fm(self.A, _c), in_=self.xT[_c * 128:(_c + 1) * 128, :]))
        dYw = p.qwait("dY")
        p.op("gpsimd", lambda e: e.iota(self.iota1p[:, 0:1], [[0, 1]], base=1, channel_multiplier=1, allow_small_or_imprecise_dtypes=True))
        gcl = p.clock("gpsimd")
        for e in ("tensor", "vector", "scalar"):
            p.op(e, lambda eng: eng.nop(), waits=dYw + [p.w("gpsimd", gcl)])

    def layernorm(self, src, dst_bf):
        p = self.p
        nc = self.nc
        with self.sbt("ln_sqs", [128, NF * 512], BF16) as sqs, \
             self.sbt("ln_bca", [128, T], F32) as bca, \
             self.sbt("ln_bcb", [128, T], F32) as bcb:
            cast_cl = [p.op("vector", lambda e, _c=c: e.tensor_copy(self.fm(dst_bf, _c), self.fm(src, _c)))
                       for c in range(NF)]
            ev_prev = 0
            mmq_prev = 0
            for nt in range(4):
                s = nt * 512
                sq_w = [p.w("tensor", mmq_prev)] if mmq_prev else []
                sq_cl = [p.op("scalar", lambda e, _c=c, _s=s: e.activation(
                    sqs[:, _c * 512:(_c + 1) * 512], self.fm(src, _c, _s, _s + 512), AF.Square),
                    waits=sq_w if c == 0 else ())
                    for c in range(NF)]
                for c in range(NF):
                    w = [p.w("vector", cast_cl[c])]
                    if c == 0 and ev_prev:
                        w.append(p.w("vector", ev_prev))
                    p.op("tensor", lambda e, _c=c, _s=s: e.matmul(
                        self.bank(0, 0, 1), self.onesb_c[:, :], self.fm(dst_bf, _c, _s, _s + 512),
                        start=(_c == 0), stop=(_c == NF - 1)), waits=w)
                for c in range(NF):
                    w = [p.w("scalar", sq_cl[c])]
                    if c == 0 and ev_prev:
                        w.append(p.w("vector", ev_prev))
                    mmq = p.op("tensor", lambda e, _c=c: e.matmul(
                        self.bank(1, 0, 1), self.onesb_c[:, :], sqs[:, _c * 512:(_c + 1) * 512],
                        start=(_c == 0), stop=(_c == NF - 1)), waits=w)
                mmq_prev = mmq
                p.op("vector", lambda e, _s=s: e.tensor_copy(self.st1[:, _s:_s + 512], self.bank(0, 0, 1)),
                     waits=[p.w("tensor", mmq)])
                ev_prev = p.op("vector", lambda e, _s=s: e.tensor_copy(
                    self.st2[:, _s:_s + 512], self.bank(1, 0, 1)))
            p.op("vector", lambda e: e.tensor_scalar(self.st1[:, :], self.st1[:, :], 1.0 / D, None, ALU.mult))
            p.op("vector", lambda e: e.tensor_scalar(self.st2[:, :], self.st2[:, :], 1.0 / D, None, ALU.mult))
            p.op("vector", lambda e: e.tensor_mul(bca[0:1, 0:T], self.st1[:, :], self.st1[:, :]))
            p.op("vector", lambda e: e.tensor_sub(self.st2[:, :], self.st2[:, :], bca[0:1, 0:T]))
            v4 = p.op("vector", lambda e: e.tensor_scalar(
                self.st2[:, :], self.st2[:, :], EPS_LN, None, ALU.add))
            s5 = p.op("scalar", lambda e: e.activation(self.st2[:, :], self.st2[:, :], AF.Sqrt),
                      waits=[p.w("vector", v4)])
            p.op("vector", lambda e: e.reciprocal(self.st2[:, :], self.st2[:, :]),
                 waits=[p.w("scalar", s5)])
            v7 = p.op("vector", lambda e: e.tensor_mul(self.st1[:, :], self.st1[:, :], self.st2[:, :]))
            eva = evb = 0
            for nt in range(4):
                s = nt * 512
                w1 = [p.w("vector", v7)] + ([p.w("vector", eva)] if eva else [])
                mm1 = p.op("tensor", lambda e, _s=s: e.matmul(
                    self.bank(0), self.ones_r[:, :], self.st2[:, _s:_s + 512], start=True, stop=True),
                    waits=w1)
                eva = p.op("vector", lambda e, _s=s: e.tensor_copy(bca[:, _s:_s + 512], self.bank(0)),
                           waits=[p.w("tensor", mm1)])
                w2 = [p.w("vector", evb)] if evb else []
                mm2 = p.op("tensor", lambda e, _s=s: e.matmul(
                    self.bank(1), self.ones_r[:, :], self.st1[:, _s:_s + 512], start=True, stop=True),
                    waits=w2)
                evb = p.op("vector", lambda e, _s=s: e.tensor_copy(bcb[:, _s:_s + 512], self.bank(1)),
                           waits=[p.w("tensor", mm2)])
            out_cl = 0
            for c in range(NF):
                p.op("vector", lambda e, _c=c: e.tensor_mul(self.fm(dst_bf, _c), self.fm(src, _c), bca[:, :]))
                out_cl = p.op("vector", lambda e, _c=c: e.tensor_sub(
                    self.fm(dst_bf, _c), self.fm(dst_bf, _c), bcb[:, :]))
            p.barrier(("tensor", "vector", "scalar"))
        return p.clock("vector")

    def attn_a(self):
        self.p.scope("attn_a")
        import contextlib
        p = self.p
        nc = self.nc
        s_h = contextlib.ExitStack()
        hbf = s_h.enter_context(self.sbt("hbf_a", [128, NF * T], BF16))
        ln_cl = self.layernorm(self.A, hbf)
        lnw = p.w("vector", ln_cl)

        s_sub = contextlib.ExitStack()
        krep = s_sub.enter_context(self.sbt("krep", [128, NKV * T], BF16))
        vtm = s_sub.enter_context(self.sbt("vtm", [128, NKV * NTT * 65], BF16))
        qqb = s_sub.enter_context(self.sbt("qqb", [128, NF * 512], BF16))
        oa = s_sub.enter_context(self.sbt("oa", [128, NF * 512], BF16))
        stg = s_sub.enter_context(self.sbt("stg", [128, 1024], F32))
        pbuf = s_sub.enter_context(self.sbt("pbuf", [128, 2 * 1024], BF16))
        rde = s_sub.enter_context(self.sbt("rde", [128, 512], F32))
        s_wq = contextlib.ExitStack()
        wq = s_wq.enter_context(self.sbt("wqa", [128, 64 * 128], BF16))
        s_kv = contextlib.ExitStack()
        wkv = s_kv.enter_context(self.sbt("wkva", [128, 16 * 128], BF16))
        vstg = s_kv.enter_context(nc.sbuf_tensor("vstg", [128, 512], BF16))

        p.op("vector", lambda e: e.memset(rde[:, :], 1.0))
        self.gen_masks("a")
        wq_t = self.load_w("aWq", wq, "dW0")
        wk_t = self.load_w("aWk", wkv, "dW1")

        k_mm_last = 0
        kv_ev = {}
        for mc in range(2):
            for s in range(0, T, 512):
                idx = mc * 4 + s // 512
                b = (2, 3)[idx % 2]
                w = [lnw, p.dw("dW1", wk_t)]
                if idx >= 2:
                    w.append(p.w("vector", kv_ev[idx - 2]))
                for kc in range(8):
                    c0 = (mc * 8 + kc) * 128
                    k_mm_last = p.op("tensor", lambda e, _b=b, _c0=c0, _kc=kc, _s=s: e.matmul(
                        self.bank(_b), wkv[:, _c0:_c0 + 128],
                        self.fm(hbf, _kc, _s, _s + 512),
                        start=(_kc == 0), stop=(_kc == 7)), waits=w if kc == 0 else ())
                for hi in range(2):
                    kvh = mc * 2 + hi
                    p.op("vector", lambda e, _b=b, _hi=hi, _k=kvh, _s=s: e.tensor_copy(
                        krep[0:64, _k * T + _s:_k * T + _s + 512],
                        self.bank(_b, _hi * 64, _hi * 64 + 64)),
                        waits=[p.w("tensor", k_mm_last)])
                    ev = p.op("vector", lambda e, _b=b, _hi=hi, _k=kvh, _s=s: e.tensor_copy(
                        krep[64:128, _k * T + _s:_k * T + _s + 512],
                        self.bank(_b, _hi * 64, _hi * 64 + 64)))
                kv_ev[idx] = ev

        wv_t = self.load_w("aWv", wkv, "dW1", waits=[p.w("tensor", k_mm_last)])

        p.op("vector", lambda e: e.memset(
            vtm[:, :].rearrange("p (s c) -> p s c", c=65)[:, :, 64:65], 1.0))

        vt_cl = 0
        vi = 0
        v_ev = {}
        for mc in range(2):
            for s in range(0, T, 512):
                b = (2, 3)[vi % 2]
                w = [p.dw("dW1", wv_t)]
                if vi >= 2:
                    w.append(p.w("vector", v_ev[vi - 2]))
                else:
                    w.append(p.w("vector", kv_ev[7]))
                for kc in range(8):
                    c0 = (mc * 8 + kc) * 128
                    mmv = p.op("tensor", lambda e, _b=b, _c0=c0, _kc=kc, _s=s: e.matmul(
                        self.bank(_b), wkv[:, _c0:_c0 + 128],
                        self.fm(hbf, _kc, _s, _s + 512),
                        start=(_kc == 0), stop=(_kc == 7)), waits=w if kc == 0 else ())
                ev = p.op("vector", lambda e, _b=b: e.tensor_copy(vstg[:, :], self.bank(_b)),
                          waits=[p.w("tensor", mmv)])
                v_ev[vi] = ev
                vi += 1
                for sub in range(4):
                    tc = s // 128 + sub
                    slot, tw = self.tslot()
                    tt = p.op("tensor", lambda e, _sub=sub, _sl=slot: e.transpose(
                        _sl[:, :], vstg[:, _sub * 128:(_sub + 1) * 128], self.ident[:, :]),
                        waits=[p.w("vector", ev)] + tw)
                    for hi in range(2):
                        kvh = mc * 2 + hi
                        vt_cl = p.op("vector", lambda e, _k=kvh, _tc=tc, _hi=hi, _sl=slot: e.tensor_copy(
                            vtm[:, (_k * NTT + _tc) * 65:(_k * NTT + _tc) * 65 + 64],
                            _sl[:, _hi * 64:_hi * 64 + 64]),
                            waits=[p.w("tensor", tt)])
                    self.tslot_done(vt_cl)
        kv_ready = p.w("vector", vt_cl)
        p.barrier(("tensor", "vector", "scalar"))
        s_kv.close()
        s_wo = contextlib.ExitStack()
        wo = s_wo.enter_context(self.sbt("woa", [128, 64 * 128], BF16))
        wo_t = self.load_w("aWo", wo, "dW2", waits=[p.w("vector", p.clock("vector"))])

        if True:
            exp_last = {0: 0, 1: 0}
            o_ev = {2: 0, 3: 0}
            for qb in range(4):
                q0 = qb * 512
                kc_lo = max(0, q0 - 1023) // 128
                kc_hi = (q0 + 511) // 128
                npair = (kc_hi - kc_lo + 1) // 2
                q_cl = 0
                qev = {}
                for mc in range(NF):
                    b = (4, 5)[mc % 2]
                    w = [lnw, p.dw("dW0", wq_t)]
                    if mc >= 2:
                        w.append(p.w("scalar", qev[mc - 2]))
                    elif exp_last[1]:
                        w.append(p.w("scalar", exp_last[1]))
                    for kc in range(NF):
                        c0 = (mc * 8 + kc) * 128
                        mmv = p.op("tensor", lambda e, _b=b, _c0=c0, _kc=kc, _q0=q0: e.matmul(
                            self.bank(_b), wq[:, _c0:_c0 + 128],
                            self.fm(hbf, _kc, _q0, _q0 + 512),
                            start=(_kc == 0), stop=(_kc == 7)), waits=w if kc == 0 else ())
                    qev[mc] = p.op("scalar", lambda e, _b=b, _mc=mc: e.copy(
                        qqb[:, _mc * 512:(_mc + 1) * 512], self.bank(_b)),
                        waits=[p.w("tensor", mmv)])
                    q_cl = qev[mc]
                q_ready = p.w("scalar", q_cl)
                exp_last[1] = q_cl

                prev_oevac = 0
                prev_rde_mm = 0
                nn_cl = 0
                for h in range(NH):
                    kv = h // 4
                    hr = (h % 2) * 64
                    hc = h // 2

                    def emit_pv(pd, last=False):
                        _kv, _kc0, _P, _tok, _first = pd
                        w = [p.w(*_tok)]
                        if _first:
                            if prev_oevac:
                                w.append(p.w("vector", prev_oevac))
                            if o_ev[2]:
                                w.append(p.w("vector", o_ev[2]))
                        p.op("tensor", lambda e, _k=_kv, _kc=_kc0, _Px=_P, _f=_first: e.matmul(
                            self.bank(2, 0, 65),
                            vtm[:, (_k * NTT + _kc) * 65:(_k * NTT + _kc + 1) * 65],
                            pbuf[:, _Px * 1024:_Px * 1024 + 512],
                            start=_f, stop=False), waits=w)
                        return p.op("tensor", lambda e, _k=_kv, _kc=_kc0 + 1, _Px=_P,
                                    _l=last: e.matmul(
                            self.bank(2, 0, 65),
                            vtm[:, (_k * NTT + _kc) * 65:(_k * NTT + _kc + 1) * 65],
                            pbuf[:, _Px * 1024 + 512:(_Px + 1) * 1024],
                            start=False, stop=_l), waits=[p.w(*_tok)])

                    pend = None
                    for j in range(npair):
                        P = j % 2
                        b0 = (0, 4)[P]
                        kc = kc_lo + 2 * j
                        w = [kv_ready, q_ready]
                        if exp_last[P]:
                            w.append(p.w("scalar", exp_last[P]))
                        p.op("tensor", lambda e, _b=b0, _kv=kv, _kc=kc, _hr=hr, _hc=hc: e.matmul(
                            self.bank(_b),
                            krep[_hr:_hr + 64, _kv * T + _kc * 128:_kv * T + (_kc + 1) * 128],
                            qqb[_hr:_hr + 64, _hc * 512:(_hc + 1) * 512],
                            start=True, stop=True), waits=w)
                        mm2 = p.op("tensor", lambda e, _b=b0 + 1, _kv=kv, _kc=kc + 1, _hr=hr,
                                   _hc=hc: e.matmul(
                            self.bank(_b),
                            krep[_hr:_hr + 64, _kv * T + _kc * 128:_kv * T + (_kc + 1) * 128],
                            qqb[_hr:_hr + 64, _hc * 512:(_hc + 1) * 512],
                            start=True, stop=True))
                        p.op("scalar", lambda e, _P=P: e.activation(
                            stg[:, :], self.ps[:, (0, 4)[_P] * 512:((0, 4)[_P] + 2) * 512],
                            AF.Tanh, scale=1.0 / 240.0), waits=[p.w("tensor", mm2)])
                        se = p.op("scalar", lambda e, _P=P: e.activation(
                            pbuf[:, _P * 1024:(_P + 1) * 1024], stg[:, :], AF.Exp, scale=30.0))
                        exp_last[P] = se
                        tok = ("scalar", se)
                        dd = kc * 128 - q0
                        msl = None
                        if dd >= 0:
                            msl = dd // 128
                        elif dd <= -640:
                            msl = 4 + (dd + 1024) // 128
                        if msl is not None:
                            vm = p.op("vector", lambda e, _P=P, _m=msl: e.tensor_mul(
                                pbuf[:, _P * 1024:(_P + 1) * 1024],
                                pbuf[:, _P * 1024:(_P + 1) * 1024],
                                self.masks[:, _m * 512:(_m + 2) * 512]),
                                waits=[p.w("scalar", se), self.mask_ready])
                            tok = ("vector", vm)
                        if pend:
                            emit_pv(pend)
                        pend = (kv, kc, P, tok, j == 0)
                    pv_last = emit_pv(pend, last=True)

                    w_d = [p.w("tensor", pv_last)]
                    if prev_rde_mm:
                        w_d.append(p.w("tensor", prev_rde_mm))
                    p.op("vector", lambda e, _h=h: e.tensor_copy(
                        rde[(_h % 2) * 64:(_h % 2) * 64 + 1, :], self.bank(2, 64, 65)), waits=w_d)
                    prev_oevac = p.op("vector", lambda e, _h=h: e.tensor_copy(
                        oa[(_h % 2) * 64:(_h % 2) * 64 + 64, (_h // 2) * 512:(_h // 2 + 1) * 512],
                        self.bank(2, 0, 64)))
                    if h % 2 == 1:
                        c = h // 2
                        p.op("vector", lambda e: e.reciprocal(rde[0:1, :], rde[0:1, :]),
                             waits=[p.w("vector", prev_oevac)])
                        p.op("vector", lambda e: e.reciprocal(rde[64:65, :], rde[64:65, :]))
                        prev_rde_mm = p.op("tensor", lambda e: e.matmul(
                            self.bank(3), self.paircol[:, :], rde[:, :],
                            start=True, stop=True),
                            waits=[p.w("vector", p.clock("vector"))])
                        nn_cl = p.op("vector", lambda e, _c=c: e.tensor_mul(
                            oa[:, _c * 512:(_c + 1) * 512], oa[:, _c * 512:(_c + 1) * 512],
                            self.bank(3)), waits=[p.w("tensor", prev_rde_mm)])
                nw = p.w("vector", nn_cl)
                for mc in range(NF):
                    b = (2, 3)[mc % 2]
                    for kc in range(NF):
                        c0 = (mc * 8 + kc) * 128
                        if kc == 0:
                            w = [nw, p.dw("dW2", wo_t)]
                            if b == 2:
                                w.append(p.w("vector", prev_oevac))
                            if o_ev[b]:
                                w.append(p.w("vector", o_ev[b]))
                        mm = p.op("tensor", lambda e, _b=b, _c0=c0, _kc=kc: e.matmul(
                            self.bank(_b), wo[:, _c0:_c0 + 128], oa[:, _kc * 512:(_kc + 1) * 512],
                            start=(_kc == 0), stop=(_kc == 7)),
                            waits=w if kc == 0 else ())
                    o_ev[b] = p.op("vector", lambda e, _b=b, _mc=mc, _q0=q0: e.tensor_tensor(
                        self.fm(self.A, _mc, _q0, _q0 + 512), self.fm(self.A, _mc, _q0, _q0 + 512),
                        self.bank(_b), ALU.add), waits=[p.w("tensor", mm)])
        self.p.barrier()
        s_wo.close()
        s_wq.close()
        s_sub.close()
        s_h.close()

    def rmsnorm(self, xc, T_, dst_bf, gcol, dst_TT=None):
        p = self.p
        nc = self.nc
        dst_TT = T_ if dst_TT is None else dst_TT
        NTB = T_ // 512
        with self.sbt("rn_sqs", [128, NF * 512], BF16) as sqs, \
             self.sbt("rn_bca", [128, T_], F32) as bca:
            ev_prev = 0
            mmq_prev = 0
            for nt in range(NTB):
                s = nt * 512
                sq_w = [p.w("tensor", mmq_prev)] if mmq_prev else []
                sq_cl = [p.op("scalar", lambda e, _c=c, _s=s: e.activation(
                    sqs[:, _c * 512:(_c + 1) * 512], xc(_c, _s, _s + 512), AF.Square),
                    waits=sq_w if c == 0 else ())
                    for c in range(NF)]
                for c in range(NF):
                    w = [p.w("scalar", sq_cl[c])]
                    if c == 0 and ev_prev:
                        w.append(p.w("vector", ev_prev))
                    mmq = p.op("tensor", lambda e, _c=c: e.matmul(
                        self.bank(1, 0, 1), self.onesb_c[:, :], sqs[:, _c * 512:(_c + 1) * 512],
                        start=(_c == 0), stop=(_c == NF - 1)), waits=w)
                mmq_prev = mmq
                ev_prev = p.op("vector", lambda e, _s=s: e.tensor_copy(
                    self.st2[:, _s:_s + 512], self.bank(1, 0, 1)), waits=[p.w("tensor", mmq)])
            veps = p.op("vector", lambda e: e.tensor_scalar(
                self.st2[:, 0:T_], self.st2[:, 0:T_], 1.0 / D, EPS_RMS, ALU.mult, ALU.add))
            s5 = p.op("scalar", lambda e: e.activation(self.st2[:, 0:T_], self.st2[:, 0:T_], AF.Sqrt),
                      waits=[p.w("vector", veps)])
            v6 = p.op("vector", lambda e: e.reciprocal(self.st2[:, 0:T_], self.st2[:, 0:T_]),
                      waits=[p.w("scalar", s5)])
            eva = 0
            for nt in range(NTB):
                s = nt * 512
                w1 = [p.w("vector", v6)] + ([p.w("vector", eva)] if eva else [])
                mm1 = p.op("tensor", lambda e, _s=s: e.matmul(
                    self.bank(0), self.ones_r[:, :], self.st2[:, _s:_s + 512], start=True, stop=True),
                    waits=w1)
                eva = p.op("vector", lambda e, _s=s: e.tensor_copy(bca[:, _s:_s + 512], self.bank(0)),
                           waits=[p.w("tensor", mm1)])
            out_cl = 0
            for c in range(NF):
                out_cl = p.op("vector", lambda e, _c=c: e.scalar_tensor_tensor(
                    dst_bf[:, _c * dst_TT:_c * dst_TT + T_], xc(_c, 0, T_),
                    self.gains[:, gcol + _c:gcol + _c + 1], bca[:, 0:T_],
                    ALU.mult, ALU.mult))
            p.barrier(("tensor", "vector", "scalar"))
        return out_cl

    def wslice(self, nm, dst, d0, fc, kind, q="dW1", waits=()):
        p = self.p
        if kind == "mcol":
            return p.dma("gpsimd", q, lambda e: e.dma_start(
                out=dst[:, d0:d0 + 1024], in_=self.W[nm][:, fc * 1024:(fc + 1) * 1024]), waits=waits)
        src = self.W[nm][:, :].rearrange("p (m k j) -> p m (k j)", k=32, j=128)[:, :, fc * 128:(fc + 1) * 128]
        return p.dma("gpsimd", q, lambda e: e.dma_start(
            out=dst[:, d0:d0 + 8 * 128].rearrange("p (m j) -> p m j", j=128), in_=src), waits=waits)

    def decoder(self, xc, T_, wqn, wkn, wvn, won, g1col, g2col, wgn, wun, wdn,
                cos_ap, sin_ap, name, stop=""):
        import contextlib
        p = self.p
        nc = self.nc
        NTB = T_ // 512
        NKC = T_ // 128
        s_hn = contextlib.ExitStack()
        hn = s_hn.enter_context(self.sbt(f"hn_{name}", [128, NF * T_], BF16))
        p.scope(f"{name}.rms1")
        self.gen_masks("dec")
        self.rmsnorm(xc, T_, hn, g1col)
        p.scope(f"{name}.qkv")

        s_at = contextlib.ExitStack()
        qb_ = s_at.enter_context(self.sbt(f"q_{name}", [128, 4 * T_], BF16))
        kb_ = s_at.enter_context(self.sbt(f"k_{name}", [128, 4 * T_], BF16))
        vtm = s_at.enter_context(self.sbt(f"vtm_{name}", [128, 8 * NKC * 65], BF16))
        oa = s_at.enter_context(self.sbt(f"oa_{name}", [128, 4 * 512], BF16))
        wo = s_at.enter_context(self.sbt(f"wo_{name}", [128, 32 * 128], BF16))
        wbuf = s_at.enter_context(self.sbt(f"w_{name}", [128, 32 * 128], BF16))
        vstg = s_at.enter_context(self.sbt(f"vstg_{name}", [128, 512], BF16))
        rstga = s_at.enter_context(self.sbt(f"rsa_{name}", [128, 512], BF16))
        rstgb = s_at.enter_context(self.sbt(f"rsb_{name}", [128, 512], BF16))
        pbuf = s_at.enter_context(self.sbt(f"pb_{name}", [128, 2 * 1024], BF16))
        rde = s_at.enter_context(self.sbt(f"rde_{name}", [128, 512], F32))
        p.op("vector", lambda e: e.memset(rde[:, :], 1.0))

        for half in range(2):
            p.scope(f"{name}.qkv")
            wo_t = p.dma("gpsimd", "dW2", lambda e, _h=half: e.dma_start(
                out=wo[:, :].rearrange("p (m kj) -> p m kj", kj=4 * 128),
                in_=self.W[won][:, :].rearrange("p (m kj) -> p m kj", kj=8 * 128)[
                    :, :, _h * 4 * 128:(_h * 4 + 4) * 128]),
                waits=[p.w("tensor", p.clock("tensor"))])
            rot_ev = {2: 0, 3: 0}
            pb_ev = {0: 0, 1: 0}

            def qk_proj(wn, dst, last_ev):
                wt = self.load_w_split(wn, wbuf, half * 32, (half + 1) * 32,
                                       waits=[p.w("tensor", p.clock("tensor"))])

                def emit_rot(pd):
                    _rb, _sl, _s, _lc, _sc = pd
                    rw = [p.w("scalar", _sc)]
                    if rot_ev[_rb]:
                        rw.append(p.w("vector", rot_ev[_rb]))
                    rmm = p.op("tensor", lambda e, __rb=_rb, __sl=_sl: e.matmul(
                        self.bank(__rb), self.rotm[:, :],
                        pbuf[:, __sl * 512:(__sl + 1) * 512],
                        start=True, stop=True), waits=rw)
                    pb_ev[_sl] = p.op("vector", lambda e, __sl=_sl, __s=_s: e.tensor_mul(
                        rstga[:, :], pbuf[:, __sl * 512:(__sl + 1) * 512], cos_ap(__s)),
                        waits=[p.w("scalar", _sc)])
                    rot_ev[_rb] = p.op("vector", lambda e, __rb=_rb, __s=_s: e.tensor_mul(
                        rstgb[:, :], self.bank(__rb), sin_ap(__s)),
                        waits=[p.w("tensor", rmm)])
                    p.op("vector", lambda e, __lc=_lc, __s=_s: e.tensor_tensor(
                        dst[:, __lc * T_ + __s:__lc * T_ + __s + 512],
                        rstga[:, :], rstgb[:, :], ALU.add))

                pend = None
                for lc in range(4):
                    for nt in range(NTB):
                        ti = lc * NTB + nt
                        s = nt * 512
                        b = (4, 5)[ti % 2]
                        w = [wt[lc]]
                        if last_ev[b]:
                            w.append(p.w(*last_ev[b]))
                        for kc in range(NF):
                            c0 = (lc * 8 + kc) * 128
                            mm = p.op("tensor", lambda e, _b=b, _c0=c0, _kc=kc, _s=s: e.matmul(
                                self.bank(_b), wbuf[:, _c0:_c0 + 128],
                                hn[:, _kc * T_ + _s:_kc * T_ + _s + 512],
                                start=(_kc == 0), stop=(_kc == 7)), waits=w if kc == 0 else ())
                        scw = [p.w("tensor", mm)]
                        if pb_ev[ti % 2]:
                            scw.append(p.w("vector", pb_ev[ti % 2]))
                        sc = p.op("scalar", lambda e, _b=b, _sl=ti % 2: e.copy(
                            pbuf[:, _sl * 512:(_sl + 1) * 512], self.bank(_b)), waits=scw)
                        if pend:
                            emit_rot(pend)
                        last_ev[b] = ("scalar", sc)
                        pend = ((2, 3)[ti % 2], ti % 2, s, lc, sc)
                emit_rot(pend)
                return last_ev

            lev = qk_proj(wqn, qb_, {4: 0, 5: 0})
            lev = qk_proj(wkn, kb_, lev)
            if stop == "qk":
                p.barrier()
                s_at.close()
                s_hn.close()
                return

            vm_cl = p.op("vector", lambda e: e.memset(
                vtm[:, :].rearrange("p (s c) -> p s c", c=65)[:, :, 64:65], 1.0))
            vi = 0
            v_ev = dict(lev)
            vt_cl = 0
            wt = self.load_w_split(wvn, wbuf, half * 32, (half + 1) * 32,
                                   waits=[p.w("tensor", p.clock("tensor"))])
            for lc in range(4):
                for nt in range(NTB):
                    s = nt * 512
                    b = (4, 5)[vi % 2]
                    w = [wt[lc]]
                    if v_ev[b]:
                        if isinstance(v_ev[b], tuple):
                            w.append(p.w(*v_ev[b]))
                        else:
                            w.append(p.w("vector", v_ev[b]))
                    for kc in range(NF):
                        c0 = (lc * 8 + kc) * 128
                        mm = p.op("tensor", lambda e, _b=b, _c0=c0, _kc=kc, _s=s: e.matmul(
                            self.bank(_b), wbuf[:, _c0:_c0 + 128],
                            hn[:, _kc * T_ + _s:_kc * T_ + _s + 512],
                            start=(_kc == 0), stop=(_kc == 7)), waits=w if kc == 0 else ())
                    ev = p.op("vector", lambda e, _b=b: e.tensor_copy(vstg[:, :], self.bank(_b)),
                              waits=[p.w("tensor", mm)])
                    v_ev[b] = ev
                    vi += 1
                    for sub in range(4):
                        tc = s // 128 + sub
                        slot, tw = self.tslot()
                        tt = p.op("tensor", lambda e, _sub=sub, _sl=slot: e.transpose(
                            _sl[:, :], vstg[:, _sub * 128:(_sub + 1) * 128], self.ident[:, :]),
                            waits=[p.w("vector", ev)] + tw)
                        for hi in range(2):
                            lh = lc * 2 + hi
                            vt_cl = p.op("vector", lambda e, _h=lh, _tc=tc, _hi=hi, _sl=slot: e.tensor_copy(
                                vtm[:, (_h * NKC + _tc) * 65:(_h * NKC + _tc) * 65 + 64],
                                _sl[:, _hi * 64:_hi * 64 + 64]),
                                waits=[p.w("tensor", tt)])
                        self.tslot_done(vt_cl)
            kv_ready = p.w("vector", vt_cl)
            p.barrier(("tensor", "vector", "scalar"))

            p.scope(f"{name}.attn")
            exp_last = {0: 0, 1: 0}
            o_ev = {2: 0, 3: 0}
            for qbi in range(NTB):
                q0 = qbi * 512
                kc_hi = (q0 + 511) // 128
                npair = (kc_hi + 1) // 2
                prev_oevac = 0
                prev_rde_mm = 0
                nn_cl = 0
                for lh in range(8):
                    hr = (lh % 2) * 64
                    hc = lh // 2

                    def emit_pv(pd, last=False):
                        _lh, _kc0, _P, _tok, _first = pd
                        w = [p.w(*_tok)]
                        if _first:
                            if prev_oevac:
                                w.append(p.w("vector", prev_oevac))
                            if o_ev.get(2):
                                w.append(p.w("vector", o_ev[2]))
                            if half == 0 and qbi == 0 and _lh == 0:
                                w.append(p.w("vector", vm_cl))
                        p.op("tensor", lambda e, _h=_lh, _kc=_kc0, _Px=_P, _f=_first: e.matmul(
                            self.bank(2, 0, 65),
                            vtm[:, (_h * NKC + _kc) * 65:(_h * NKC + _kc + 1) * 65],
                            pbuf[:, _Px * 1024:_Px * 1024 + 512],
                            start=_f, stop=False), waits=w)
                        return p.op("tensor", lambda e, _h=_lh, _kc=_kc0 + 1, _Px=_P,
                                    _l=last: e.matmul(
                            self.bank(2, 0, 65),
                            vtm[:, (_h * NKC + _kc) * 65:(_h * NKC + _kc + 1) * 65],
                            pbuf[:, _Px * 1024 + 512:(_Px + 1) * 1024],
                            start=False, stop=_l), waits=[p.w(*_tok)])

                    pend = None
                    for j in range(npair):
                        P = j % 2
                        b0 = (0, 4)[P]
                        kc = 2 * j
                        w = [kv_ready]
                        if exp_last[P]:
                            w.append(p.w("scalar", exp_last[P]))
                        p.op("tensor", lambda e, _b=b0, _kc=kc, _hr=hr, _hc=hc, _q0=q0: e.matmul(
                            self.bank(_b),
                            kb_[_hr:_hr + 64, _hc * T_ + _kc * 128:_hc * T_ + (_kc + 1) * 128],
                            qb_[_hr:_hr + 64, _hc * T_ + _q0:_hc * T_ + _q0 + 512],
                            start=True, stop=True), waits=w)
                        mm2 = p.op("tensor", lambda e, _b=b0 + 1, _kc=kc + 1, _hr=hr,
                                   _hc=hc, _q0=q0: e.matmul(
                            self.bank(_b),
                            kb_[_hr:_hr + 64, _hc * T_ + _kc * 128:_hc * T_ + (_kc + 1) * 128],
                            qb_[_hr:_hr + 64, _hc * T_ + _q0:_hc * T_ + _q0 + 512],
                            start=True, stop=True))
                        se = p.op("scalar", lambda e, _P=P: e.activation(
                            pbuf[:, _P * 1024:(_P + 1) * 1024],
                            self.ps[:, (0, 4)[_P] * 512:((0, 4)[_P] + 2) * 512],
                            AF.Exp, scale=0.125), waits=[p.w("tensor", mm2)])
                        exp_last[P] = se
                        tok = ("scalar", se)
                        dd = kc * 128 - q0
                        if dd >= 0:
                            vm = p.op("vector", lambda e, _P=P, _m=dd // 128: e.tensor_mul(
                                pbuf[:, _P * 1024:(_P + 1) * 1024],
                                pbuf[:, _P * 1024:(_P + 1) * 1024],
                                self.masks[:, _m * 512:(_m + 2) * 512]),
                                waits=[p.w("scalar", se), self.mask_ready])
                            tok = ("vector", vm)
                        if pend:
                            emit_pv(pend)
                        pend = (lh, kc, P, tok, j == 0)
                    pv_last = emit_pv(pend, last=True)

                    w_d = [p.w("tensor", pv_last)]
                    if prev_rde_mm:
                        w_d.append(p.w("tensor", prev_rde_mm))
                    p.op("vector", lambda e, _h=lh: e.tensor_copy(
                        rde[(_h % 2) * 64:(_h % 2) * 64 + 1, :], self.bank(2, 64, 65)), waits=w_d)
                    prev_oevac = p.op("vector", lambda e, _h=lh: e.tensor_copy(
                        oa[(_h % 2) * 64:(_h % 2) * 64 + 64, (_h // 2) * 512:(_h // 2 + 1) * 512],
                        self.bank(2, 0, 64)))
                    if lh % 2 == 1:
                        c = lh // 2
                        p.op("vector", lambda e: e.reciprocal(rde[0:1, :], rde[0:1, :]),
                             waits=[p.w("vector", prev_oevac)])
                        p.op("vector", lambda e: e.reciprocal(rde[64:65, :], rde[64:65, :]))
                        prev_rde_mm = p.op("tensor", lambda e: e.matmul(
                            self.bank(3), self.paircol[:, :], rde[:, :], start=True, stop=True),
                            waits=[p.w("vector", p.clock("vector"))])
                        nn_cl = p.op("vector", lambda e, _c=c: e.tensor_mul(
                            oa[:, _c * 512:(_c + 1) * 512], oa[:, _c * 512:(_c + 1) * 512],
                            self.bank(3)), waits=[p.w("tensor", prev_rde_mm)])
                nw = p.w("vector", nn_cl)
                for mc in range(NF):
                    b = (2, 3)[mc % 2]
                    for lc in range(4):
                        c0 = (mc * 4 + lc) * 128
                        if lc == 0:
                            w = [nw, p.dw("dW2", wo_t)]
                            if b == 2:
                                w.append(p.w("vector", prev_oevac))
                            if o_ev.get(b):
                                w.append(p.w("vector", o_ev[b]))
                        mm = p.op("tensor", lambda e, _b=b, _c0=c0, _lc=lc: e.matmul(
                            self.bank(_b), wo[:, _c0:_c0 + 128], oa[:, _lc * 512:(_lc + 1) * 512],
                            start=(_lc == 0), stop=(_lc == 3)),
                            waits=w if lc == 0 else ())
                    o_ev[b] = p.op("vector", lambda e, _b=b, _mc=mc, _q0=q0: e.tensor_tensor(
                        xc(_mc, _q0, _q0 + 512), xc(_mc, _q0, _q0 + 512),
                        self.bank(_b), ALU.add), waits=[p.w("tensor", mm)])
            p.barrier(("tensor", "vector", "scalar"))
        p.barrier()
        s_at.close()
        s_hn.close()
        if stop == "attn":
            return

        p.scope(f"{name}.rms2")
        s_sg = contextlib.ExitStack()
        h2n = s_sg.enter_context(self.sbt(f"h2_{name}", [128, NF * T_], BF16))
        self.rmsnorm(xc, T_, h2n, g2col)
        p.scope(f"{name}.swiglu")
        sbuf_s = s_sg.enter_context(self.sbt(f"s_{name}", [128, NFC * 512], BF16))
        wgu = s_sg.enter_context(self.sbt(f"wgu_{name}", [128, 2 * 2048], BF16))
        wdd = s_sg.enter_context(self.sbt(f"wd_{name}", [128, 2 * 4096], BF16))
        sg1 = s_sg.enter_context(self.sbt(f"sg_{name}", [128, 512], BF16))
        for tb in range(NTB):
            s = tb * 512
            gu_ev = {}
            gu_mm = {}
            for fc in range(NFC):
                sl = fc % 2
                wt = self.wslice(wgn, wgu, sl * 2048, fc, "mcol", "dW1",
                                 waits=[p.w("tensor", gu_mm[fc - 2])] if fc >= 2 else ())
                wt2 = self.wslice(wun, wgu, sl * 2048 + 1024, fc, "mcol", "dW1")
                w = [wt, wt2]
                if fc >= 2:
                    w.append(p.w("vector", gu_ev[fc - 2]))
                for kc in range(NF):
                    p.op("tensor", lambda e, _sl=sl, _kc=kc, _s=s: e.matmul(
                        self.bank(0 + _sl), wgu[:, _sl * 2048 + _kc * 128:_sl * 2048 + _kc * 128 + 128],
                        h2n[:, _kc * T_ + _s:_kc * T_ + _s + 512],
                        start=(_kc == 0), stop=(_kc == 7)), waits=w if kc == 0 else ())
                for kc in range(NF):
                    mmu = p.op("tensor", lambda e, _sl=sl, _kc=kc, _s=s: e.matmul(
                        self.bank(2 + _sl), wgu[:, _sl * 2048 + 1024 + _kc * 128:_sl * 2048 + 1024 + _kc * 128 + 128],
                        h2n[:, _kc * T_ + _s:_kc * T_ + _s + 512],
                        start=(_kc == 0), stop=(_kc == 7)))
                gu_mm[fc] = mmu
                sw = [p.w("tensor", mmu)]
                if fc >= 1:
                    sw.append(p.w("vector", gu_ev[fc - 1]))
                sv = p.op("scalar", lambda e, _sl=sl: e.activation(
                    sg1[:, :], self.bank(0 + _sl), AF.Silu), waits=sw)
                gu_ev[fc] = p.op("vector", lambda e, _sl=sl, _fc=fc: e.tensor_tensor(
                    sbuf_s[:, _fc * 512:(_fc + 1) * 512], sg1[:, :], self.bank(2 + _sl), ALU.mult),
                    waits=[p.w("scalar", sv)])
            wd_ev = {4: 0, 5: 0}
            wd_mm = {}
            for mc in range(NF):
                b = (4, 5)[mc % 2]
                sl = mc % 2
                wt = self.load_w(wdn, wdd, "dW3", c0=mc * 32, c1=(mc + 1) * 32, d0=sl * 4096,
                                 waits=[p.w("tensor", wd_mm[mc - 2])] if mc >= 2 else ())
                w0 = [p.dw("dW3", wt), p.w("vector", gu_ev[NFC - 1]), p.w("vector", gu_ev[NFC - 2])]
                if wd_ev[b]:
                    w0.append(p.w("vector", wd_ev[b]))
                for fc in range(NFC):
                    mmw = p.op("tensor", lambda e, _b=b, _sl=sl, _fc=fc: e.matmul(
                        self.bank(_b), wdd[:, _sl * 4096 + _fc * 128:_sl * 4096 + (_fc + 1) * 128],
                        sbuf_s[:, _fc * 512:(_fc + 1) * 512],
                        start=(_fc == 0), stop=(_fc == NFC - 1)), waits=w0 if fc == 0 else ())
                wd_mm[mc] = mmw
                wd_ev[b] = p.op("vector", lambda e, _b=b, _mc=mc, _s=s: e.tensor_tensor(
                    xc(_mc, _s, _s + 512), xc(_mc, _s, _s + 512), self.bank(_b), ALU.add),
                    waits=[p.w("tensor", mmw)])
            p.barrier(("tensor", "vector", "scalar", "gpsimd"))
        s_sg.close()
        p.barrier(("tensor", "vector", "scalar"))

    def attn_b(self):
        self.p.scope("attn_b")
        import contextlib
        p = self.p
        nc = self.nc
        s_h = contextlib.ExitStack()
        hbf = s_h.enter_context(self.sbt("hbf_b", [128, NF * T], BF16))
        self.gen_masks("b")
        ln_cl = self.layernorm(self.A, hbf)
        lnw = p.w("vector", ln_cl)

        s_m = contextlib.ExitStack()
        m1 = s_m.enter_context(self.sbt("m1_b", [128, NF * T], BF16))
        wb1 = s_m.enter_context(self.sbt("w1_b", [128, 64 * 128], BF16))
        wt = self.load_w("bW1", wb1, "dW0")
        ev = {4: 0, 5: 0}
        for mc in range(NF):
            for nt in range(4):
                s = nt * 512
                b = (4, 5)[(mc * 4 + nt) % 2]
                w = [lnw, p.dw("dW0", wt)]
                if ev[b]:
                    w.append(p.w("scalar", ev[b]))
                for kc in range(NF):
                    c0 = (mc * 8 + kc) * 128
                    mm = p.op("tensor", lambda e, _b=b, _c0=c0, _kc=kc, _s=s: e.matmul(
                        self.bank(_b), wb1[:, _c0:_c0 + 128], hbf[:, _kc * T + _s:_kc * T + _s + 512],
                        start=(_kc == 0), stop=(_kc == 7)), waits=w if kc == 0 else ())
                ev[b] = p.op("scalar", lambda e, _b=b, _mc=mc, _s=s: e.activation(
                    m1[:, _mc * T + _s:_mc * T + _s + 512], self.bank(_b), AF.Silu,
                    bias=self.biases[:, _mc:_mc + 1]), waits=[p.w("tensor", mm)])
        mlp1_cl = max(ev[4], ev[5])
        wt = self.load_w("bW2", wb1, "dW0", waits=[p.w("tensor", p.clock("tensor"))])
        ev = {4: 0, 5: 0}
        for mc in range(NF):
            for nt in range(4):
                s = nt * 512
                b = (4, 5)[(mc * 4 + nt) % 2]
                w = [p.w("scalar", mlp1_cl), p.dw("dW0", wt)]
                if ev[b]:
                    w.append(p.w("vector", ev[b]))
                for kc in range(NF):
                    c0 = (mc * 8 + kc) * 128
                    mm = p.op("tensor", lambda e, _b=b, _c0=c0, _kc=kc, _s=s: e.matmul(
                        self.bank(_b), wb1[:, _c0:_c0 + 128], m1[:, _kc * T + _s:_kc * T + _s + 512],
                        start=(_kc == 0), stop=(_kc == 7)), waits=w if kc == 0 else ())
                ev[b] = p.op("vector", lambda e, _b=b, _mc=mc, _s=s: e.tensor_scalar(
                    hbf[:, _mc * T + _s:_mc * T + _s + 512], self.bank(_b),
                    self.biases[:, 8 + _mc:8 + _mc + 1], None, ALU.add),
                    waits=[p.w("tensor", mm)])
        h_cl = max(ev[4], ev[5])
        p.barrier(("tensor", "vector", "scalar", "gpsimd"))
        s_m.close()

        s_at = contextlib.ExitStack()
        qb_ = s_at.enter_context(self.sbt("q_b", [128, 4 * T], BF16))
        kb_ = s_at.enter_context(self.sbt("k_b", [128, 4 * T], BF16))
        vtm = s_at.enter_context(self.sbt("vtm_b", [128, 8 * NTT * 65], BF16))
        oa = s_at.enter_context(self.sbt("oa_b", [128, 4 * 512], BF16))
        wo = s_at.enter_context(self.sbt("wo_b", [128, 32 * 128], BF16))
        vstg = s_at.enter_context(self.sbt("vstg_b", [128, 512], BF16))
        pbuf = s_at.enter_context(self.sbt("pb_b", [128, 2 * 1024], BF16))
        rde = s_at.enter_context(self.sbt("rde_b", [128, 512], F32))
        s_w2 = contextlib.ExitStack()
        wb = s_w2.enter_context(self.sbt("w2_b", [128, 32 * 128], BF16))
        p.op("vector", lambda e: e.memset(rde[:, :], 1.0))

        for half in range(2):
            wo_t = p.dma("gpsimd", "dW2", lambda e, _h=half: e.dma_start(
                out=wo[:, :].rearrange("p (m kj) -> p m kj", kj=4 * 128),
                in_=self.W["bWo"][:, :].rearrange("p (m kj) -> p m kj", kj=8 * 128)[
                    :, :, _h * 4 * 128:(_h * 4 + 4) * 128]),
                waits=[p.w("tensor", p.clock("tensor"))])

            def gproj(img_mc0, dst, bcol0, last_ev):
                wt = self.load_w_split("bWg", wb, (img_mc0 + half * 4) * 8,
                                       (img_mc0 + half * 4 + 4) * 8,
                                       waits=[p.w("tensor", p.clock("tensor"))])
                ev = dict(last_ev)
                for lc in range(4):
                    gmc = half * 4 + lc
                    for nt in range(4):
                        s = nt * 512
                        b = (4, 5)[(lc * 4 + nt) % 2]
                        w = [p.w("vector", h_cl), wt[lc]]
                        if ev[b]:
                            w.append(p.w("vector", ev[b]))
                        for kc in range(NF):
                            c0 = (lc * 8 + kc) * 128
                            mm = p.op("tensor", lambda e, _b=b, _c0=c0, _kc=kc, _s=s: e.matmul(
                                self.bank(_b), wb[:, _c0:_c0 + 128],
                                hbf[:, _kc * T + _s:_kc * T + _s + 512],
                                start=(_kc == 0), stop=(_kc == 7)), waits=w if kc == 0 else ())
                        ev[b] = p.op("vector", lambda e, _b=b, _lc=lc, _s=s, _g=gmc, _b0=bcol0: e.tensor_scalar(
                            dst[:, _lc * T + _s:_lc * T + _s + 512], self.bank(_b),
                            self.biases[:, 16 + _b0 + _g:16 + _b0 + _g + 1], None, ALU.add),
                            waits=[p.w("tensor", mm)])
                return ev

            lev = gproj(0, qb_, 0, {4: 0, 5: 0})
            lev = gproj(8, kb_, 8, lev)
            vm_cl = p.op("vector", lambda e: e.memset(
                vtm[:, :].rearrange("p (s c) -> p s c", c=65)[:, :, 64:65], 1.0))
            wt = self.load_w_split("bWg", wb, (16 + half * 4) * 8, (16 + half * 4 + 4) * 8,
                                   waits=[p.w("tensor", p.clock("tensor"))])
            vt_cl = 0
            v_ev = dict(lev)
            vi = 0
            for lc in range(4):
                gmc = half * 4 + lc
                for nt in range(4):
                    s = nt * 512
                    b = (4, 5)[vi % 2]
                    w = [p.w("vector", h_cl), wt[lc]]
                    if v_ev[b]:
                        w.append(p.w("vector", v_ev[b]))
                    for kc in range(NF):
                        c0 = (lc * 8 + kc) * 128
                        mm = p.op("tensor", lambda e, _b=b, _c0=c0, _kc=kc, _s=s: e.matmul(
                            self.bank(_b), wb[:, _c0:_c0 + 128],
                            hbf[:, _kc * T + _s:_kc * T + _s + 512],
                            start=(_kc == 0), stop=(_kc == 7)), waits=w if kc == 0 else ())
                    vev = p.op("vector", lambda e, _b=b, _g=gmc: e.tensor_scalar(
                        vstg[:, :], self.bank(_b),
                        self.biases[:, 16 + 16 + _g:16 + 16 + _g + 1], None, ALU.add),
                        waits=[p.w("tensor", mm)])
                    v_ev[b] = vev
                    vi += 1
                    for sub in range(4):
                        tc = s // 128 + sub
                        slot, tw = self.tslot()
                        tt = p.op("tensor", lambda e, _sub=sub, _sl=slot: e.transpose(
                            _sl[:, :], vstg[:, _sub * 128:(_sub + 1) * 128], self.ident[:, :]),
                            waits=[p.w("vector", vev)] + tw)
                        for hi in range(2):
                            lh = lc * 2 + hi
                            vt_cl = p.op("vector", lambda e, _h=lh, _tc=tc, _hi=hi, _sl=slot: e.tensor_copy(
                                vtm[:, (_h * NTT + _tc) * 65:(_h * NTT + _tc) * 65 + 64],
                                _sl[:, _hi * 64:_hi * 64 + 64]),
                                waits=[p.w("tensor", tt)])
                        self.tslot_done(vt_cl)
            kv_ready = p.w("vector", vt_cl)
            p.barrier(("tensor", "vector", "scalar"))

            exp_last = {0: 0, 1: 0}
            o_ev = {2: 0, 3: 0}
            for qbi in range(4):
                q0 = qbi * 512
                kc_hi = min(NTT - 1, (q0 + 511 + 63) // 128)
                nk = kc_hi + 1
                npair = nk // 2
                tail = nk % 2
                prev_oevac = 0
                prev_rde_mm = 0
                nn_cl = 0
                for lh in range(8):
                    hr = (lh % 2) * 64
                    hc = lh // 2

                    def emit_pv(pd, last=False):
                        _lh, _kc0, _P, _tok, _first, _single = pd
                        w = [p.w(*_tok)]
                        if _first:
                            if prev_oevac:
                                w.append(p.w("vector", prev_oevac))
                            if o_ev[2]:
                                w.append(p.w("vector", o_ev[2]))
                            if qbi == 0 and _lh == 0:
                                w.append(p.w("vector", vm_cl))
                        pv = p.op("tensor", lambda e, _h=_lh, _kc=_kc0, _Px=_P, _f=_first,
                                  _l=(last and _single): e.matmul(
                            self.bank(2, 0, 65),
                            vtm[:, (_h * NTT + _kc) * 65:(_h * NTT + _kc + 1) * 65],
                            pbuf[:, _Px * 1024:_Px * 1024 + 512],
                            start=_f, stop=_l), waits=w)
                        if _single:
                            return pv
                        return p.op("tensor", lambda e, _h=_lh, _kc=_kc0 + 1, _Px=_P,
                                    _l=last: e.matmul(
                            self.bank(2, 0, 65),
                            vtm[:, (_h * NTT + _kc) * 65:(_h * NTT + _kc + 1) * 65],
                            pbuf[:, _Px * 1024 + 512:(_Px + 1) * 1024],
                            start=False, stop=_l), waits=[p.w(*_tok)])

                    pend = None
                    for j in range(npair + tail):
                        single = (j == npair)
                        P = j % 2
                        b0 = (0, 4)[P]
                        kc = 2 * j
                        w = [kv_ready]
                        if exp_last[P]:
                            w.append(p.w("scalar", exp_last[P]))
                        p.op("tensor", lambda e, _b=b0, _kc=kc, _hr=hr, _hc=hc, _q0=q0: e.matmul(
                            self.bank(_b),
                            kb_[_hr:_hr + 64, _hc * T + _kc * 128:_hc * T + (_kc + 1) * 128],
                            qb_[_hr:_hr + 64, _hc * T + _q0:_hc * T + _q0 + 512],
                            start=True, stop=True), waits=w)
                        mm2 = p.clock("tensor")
                        if not single:
                            mm2 = p.op("tensor", lambda e, _b=b0 + 1, _kc=kc + 1, _hr=hr,
                                       _hc=hc, _q0=q0: e.matmul(
                                self.bank(_b),
                                kb_[_hr:_hr + 64, _hc * T + _kc * 128:_hc * T + (_kc + 1) * 128],
                                qb_[_hr:_hr + 64, _hc * T + _q0:_hc * T + _q0 + 512],
                                start=True, stop=True))
                        wid = 512 if single else 1024
                        se = p.op("scalar", lambda e, _P=P, _w=wid: e.activation(
                            pbuf[:, _P * 1024:_P * 1024 + _w],
                            self.ps[:, (0, 4)[_P] * 512:(0, 4)[_P] * 512 + _w],
                            AF.Exp, scale=0.125), waits=[p.w("tensor", mm2)])
                        exp_last[P] = se
                        tok = ("scalar", se)
                        dd = kc * 128 - q0
                        if dd >= 0:
                            vm = p.op("vector", lambda e, _P=P, _m=dd // 128, _w=wid: e.tensor_mul(
                                pbuf[:, _P * 1024:_P * 1024 + _w],
                                pbuf[:, _P * 1024:_P * 1024 + _w],
                                self.masks[:, _m * 512:_m * 512 + _w]),
                                waits=[p.w("scalar", se), self.mask_ready])
                            tok = ("vector", vm)
                        if pend:
                            emit_pv(pend)
                        pend = (lh, kc, P, tok, j == 0, single)
                    pv_last = emit_pv(pend, last=True)

                    w_d = [p.w("tensor", pv_last)]
                    if prev_rde_mm:
                        w_d.append(p.w("tensor", prev_rde_mm))
                    p.op("vector", lambda e, _h=lh: e.tensor_copy(
                        rde[(_h % 2) * 64:(_h % 2) * 64 + 1, :], self.bank(2, 64, 65)), waits=w_d)
                    prev_oevac = p.op("vector", lambda e, _h=lh: e.tensor_copy(
                        oa[(_h % 2) * 64:(_h % 2) * 64 + 64, (_h // 2) * 512:(_h // 2 + 1) * 512],
                        self.bank(2, 0, 64)))
                    if lh % 2 == 1:
                        c = lh // 2
                        p.op("vector", lambda e: e.reciprocal(rde[0:1, :], rde[0:1, :]),
                             waits=[p.w("vector", prev_oevac)])
                        p.op("vector", lambda e: e.reciprocal(rde[64:65, :], rde[64:65, :]))
                        prev_rde_mm = p.op("tensor", lambda e: e.matmul(
                            self.bank(3), self.paircol[:, :], rde[:, :], start=True, stop=True),
                            waits=[p.w("vector", p.clock("vector"))])
                        nn_cl = p.op("vector", lambda e, _c=c: e.tensor_mul(
                            oa[:, _c * 512:(_c + 1) * 512], oa[:, _c * 512:(_c + 1) * 512],
                            self.bank(3)), waits=[p.w("tensor", prev_rde_mm)])
                nw = p.w("vector", nn_cl)
                for mc in range(NF):
                    b = (2, 3)[mc % 2]
                    for lc in range(4):
                        c0 = (mc * 4 + lc) * 128
                        if lc == 0:
                            w = [nw, p.dw("dW2", wo_t)]
                            if b == 2:
                                w.append(p.w("vector", prev_oevac))
                            if o_ev[b]:
                                w.append(p.w("vector", o_ev[b]))
                        mm = p.op("tensor", lambda e, _b=b, _c0=c0, _lc=lc: e.matmul(
                            self.bank(_b), wo[:, _c0:_c0 + 128], oa[:, _lc * 512:(_lc + 1) * 512],
                            start=(_lc == 0), stop=(_lc == 3)),
                            waits=w if lc == 0 else ())
                    if half == 0:
                        o_ev[b] = p.op("vector", lambda e, _b=b, _mc=mc, _q0=q0: e.scalar_tensor_tensor(
                            self.fm(self.A, _mc, _q0, _q0 + 512), self.bank(_b),
                            self.biases[:, 40 + _mc:40 + _mc + 1],
                            self.fm(self.A, _mc, _q0, _q0 + 512),
                            ALU.add, ALU.add), waits=[p.w("tensor", mm)])
                    else:
                        o_ev[b] = p.op("vector", lambda e, _b=b, _mc=mc, _q0=q0: e.tensor_tensor(
                            self.fm(self.A, _mc, _q0, _q0 + 512),
                            self.fm(self.A, _mc, _q0, _q0 + 512),
                            self.bank(_b), ALU.add), waits=[p.w("tensor", mm)])
            p.barrier(("tensor", "vector", "scalar"))
        s_w2.close()
        s_at.close()
        s_h.close()
        self.p.barrier()

    def layernorm32(self, pre_waits):
        p = self.p
        nc = self.nc
        xc = lambda c, t0, t1: self.fm(self.A, c, t0, t1)
        with self.sbt("l3_xbs", [128, NF * 512], BF16) as xbs, \
             self.sbt("l3_sqs", [128, NF * 512], BF16) as sqs, \
             self.sbt("l3_bca", [128, T], F32) as bca, \
             self.sbt("l3_bcb", [128, T], F32) as bcb:
            ev_prev = 0
            mm_prev = 0
            for nt in range(4):
                s = nt * 512
                cw = [p.w("tensor", mm_prev)] if mm_prev else []
                cast_cl = [p.op("vector", lambda e, _c=c, _s=s: e.tensor_copy(
                    xbs[:, _c * 512:(_c + 1) * 512], xc(_c, _s, _s + 512)),
                    waits=cw if c == 0 else ()) for c in range(NF)]
                sq_cl = [p.op("scalar", lambda e, _c=c, _s=s: e.activation(
                    sqs[:, _c * 512:(_c + 1) * 512], xc(_c, _s, _s + 512), AF.Square),
                    waits=cw if c == 0 else ()) for c in range(NF)]
                for c in range(NF):
                    w = [p.w("vector", cast_cl[c])]
                    if c == 0 and ev_prev:
                        w.append(p.w("vector", ev_prev))
                    p.op("tensor", lambda e, _c=c: e.matmul(
                        self.bank(0, 0, 1), self.onesb_c[:, :], xbs[:, _c * 512:(_c + 1) * 512],
                        start=(_c == 0), stop=(_c == NF - 1)), waits=w)
                for c in range(NF):
                    w = [p.w("scalar", sq_cl[c])]
                    if c == 0 and ev_prev:
                        w.append(p.w("vector", ev_prev))
                    mmq = p.op("tensor", lambda e, _c=c: e.matmul(
                        self.bank(1, 0, 1), self.onesb_c[:, :], sqs[:, _c * 512:(_c + 1) * 512],
                        start=(_c == 0), stop=(_c == NF - 1)), waits=w)
                mm_prev = mmq
                p.op("vector", lambda e, _s=s: e.tensor_copy(self.st1[:, _s:_s + 512], self.bank(0, 0, 1)),
                     waits=[p.w("tensor", mmq)])
                ev_prev = p.op("vector", lambda e, _s=s: e.tensor_copy(
                    self.st2[:, _s:_s + 512], self.bank(1, 0, 1)))
            p.op("vector", lambda e: e.tensor_scalar(self.st1[:, :], self.st1[:, :], 1.0 / D, None, ALU.mult))
            p.op("vector", lambda e: e.tensor_scalar(self.st2[:, :], self.st2[:, :], 1.0 / D, None, ALU.mult))
            p.op("vector", lambda e: e.tensor_mul(bca[0:1, 0:T], self.st1[:, :], self.st1[:, :]))
            p.op("vector", lambda e: e.tensor_sub(self.st2[:, :], self.st2[:, :], bca[0:1, 0:T]))
            v4 = p.op("vector", lambda e: e.tensor_scalar(
                self.st2[:, :], self.st2[:, :], EPS_LN, None, ALU.add))
            s5 = p.op("scalar", lambda e: e.activation(self.st2[:, :], self.st2[:, :], AF.Sqrt),
                      waits=[p.w("vector", v4)])
            p.op("vector", lambda e: e.reciprocal(self.st2[:, :], self.st2[:, :]),
                 waits=[p.w("scalar", s5)])
            v7 = p.op("vector", lambda e: e.tensor_mul(self.st1[:, :], self.st1[:, :], self.st2[:, :]))
            eva = evb = 0
            for nt in range(4):
                s = nt * 512
                w1 = [p.w("vector", v7)] + ([p.w("vector", eva)] if eva else [])
                mm1 = p.op("tensor", lambda e, _s=s: e.matmul(
                    self.bank(0), self.ones_r[:, :], self.st2[:, _s:_s + 512], start=True, stop=True),
                    waits=w1)
                eva = p.op("vector", lambda e, _s=s: e.tensor_copy(bca[:, _s:_s + 512], self.bank(0)),
                           waits=[p.w("tensor", mm1)])
                w2 = [p.w("vector", evb)] if evb else []
                mm2 = p.op("tensor", lambda e, _s=s: e.matmul(
                    self.bank(1), self.ones_r[:, :], self.st1[:, _s:_s + 512], start=True, stop=True),
                    waits=w2)
                evb = p.op("vector", lambda e, _s=s: e.tensor_copy(bcb[:, _s:_s + 512], self.bank(1)),
                           waits=[p.w("tensor", mm2)])
            for c in range(NF):
                p.op("vector", lambda e, _c=c: e.tensor_mul(xc(_c, 0, T), xc(_c, 0, T), bca[:, :]),
                     waits=list(pre_waits))
                p.op("vector", lambda e, _c=c: e.tensor_sub(xc(_c, 0, T), xc(_c, 0, T), bcb[:, :]))
            p.barrier(("tensor", "vector", "scalar"))

    def ffn(self, call, stop_after=""):
        import contextlib
        p = self.p
        nc = self.nc
        xc = lambda c, t0, t1: self.fm(self.A, c, t0, t1)
        p.scope(f"ffn{call}.ln")
        for c in range(NF):
            p.dma("sync", "dX", lambda e, _c=c: e.dma_start(
                out=self.spill_x[_c * 128:(_c + 1) * 128, :], in_=self.fm(self.A, _c)),
                waits=[p.w("vector", p.clock("vector")), p.w("tensor", p.clock("tensor"))])
        spx_w = p.qwait("dX")
        self.layernorm32(spx_w)
        for c in range(NF):
            p.dma("sync", "dX", lambda e, _c=c: e.dma_start(
                out=self.spill_h[_c * 128:(_c + 1) * 128, :], in_=self.fm(self.A, _c)),
                waits=[p.w("vector", p.clock("vector"))])
        sph_w = p.qwait("dX")
        p.scope(f"ffn{call}.prior")
        with self.sbt("pr_hn", [128, NF * T], BF16) as hnp, \
             self.sbt("pr_w", [128, 64 * 128], BF16) as wp, \
             self.sbt("pr_st", [128, 2 * 512], BF16) as pstg:
            self.rmsnorm(xc, T, hnp, 32)
            pev = {0: 0, 1: 0}
            for half in range(2):
                wt = self.load_w_split("fPw", wp, half * 64, (half + 1) * 64,
                                       waits=[p.w("tensor", p.clock("tensor"))])
                tgt = (self.spill_mu, self.spill_lv)[half]
                for mc in range(8):
                    for nt in range(4):
                        s = nt * 512
                        b = (4, 5)[(mc * 4 + nt) % 2]
                        sl = (mc * 4 + nt) % 2
                        w = [wt[mc // 2]]
                        if pev[sl]:
                            w.append(p.dw("dX", pev[sl]))
                        for kc in range(NF):
                            c0 = (mc * 8 + kc) * 128
                            mm = p.op("tensor", lambda e, _b=b, _c0=c0, _kc=kc, _s=s: e.matmul(
                                self.bank(_b), wp[:, _c0:_c0 + 128],
                                hnp[:, _kc * T + _s:_kc * T + _s + 512],
                                start=(_kc == 0), stop=(_kc == 7)), waits=w if kc == 0 else ())
                        ev = p.op("vector", lambda e, _b=b, _sl=sl: e.tensor_copy(
                            pstg[:, _sl * 512:(_sl + 1) * 512], self.bank(_b)),
                            waits=[p.w("tensor", mm)])
                        pev[sl] = p.dma("sync", "dX", lambda e, _t=tgt, _mc=mc, _s=s, _sl=sl: e.dma_start(
                            out=_t[_mc * 128:(_mc + 1) * 128, _s:_s + 512],
                            in_=pstg[:, _sl * 512:(_sl + 1) * 512]),
                            waits=[p.w("vector", ev)])
            pr_w = p.qwait("dX")
            p.barrier()
        if stop_after == "prior":
            return
        with self.sbt("cosr", [128, T], BF16) as cosr, \
             self.sbt("sinr", [128, T], BF16) as sinr:
            cs_t1 = p.dma("sync", "dY", lambda e: e.dma_start(out=cosr[:, :], in_=self.CT["cos_rep"][:, :]))
            cs_t2 = p.dma("sync", "dY", lambda e: e.dma_start(out=sinr[:, :], in_=self.CT["sin_rep"][:, :]))
            p.op("vector", lambda e: e.nop(), waits=[cs_t1, cs_t2])
            self.decoder(xc, T, "fWq", "fWk", "fWv", "fWo", 0, 8, "fWg", "fWu", "fWd",
                         lambda s: cosr[:, s:s + 512], lambda s: sinr[:, s:s + 512],
                         f"d1_{call}",
                         stop={"dattn": "attn", "dqk": "qk"}.get(stop_after, ""))
        if stop_after in ("dec1", "dattn", "dqk"):
            return
        self.routing(call, sph_w, pr_w)
        p.scope(f"ffn{call}.restream")
        with self.sbt("xrs", [128, 2 * T], F32) as xrs:
            rs_ev = {0: 0, 1: 0}
            for c in range(NF):
                sl = c % 2
                w = [p.w("vector", p.clock("vector"))]
                if rs_ev[sl]:
                    w.append(p.w("vector", rs_ev[sl]))
                dt = p.dma("sync", "dX", lambda e, _c=c, _sl=sl: e.dma_start(
                    out=xrs[:, _sl * T:(_sl + 1) * T], in_=self.spill_x[_c * 128:(_c + 1) * 128, :]),
                    waits=w)
                rs_ev[sl] = p.op("vector", lambda e, _c=c, _sl=sl: e.tensor_tensor(
                    xc(_c, 0, T), xc(_c, 0, T), xrs[:, _sl * T:(_sl + 1) * T], ALU.add),
                    waits=[p.dw("dX", dt)])
            p.barrier()

    def routing(self, call, sph_w, pr_w):
        import contextlib
        p = self.p
        nc = self.nc
        xc = lambda c, t0, t1: self.fm(self.A, c, t0, t1)
        t = self.sbt
        s_keep = contextlib.ExitStack()
        mcs = s_keep.enter_context(t("rt_mcs", [128, 16], F32))
        rcs = s_keep.enter_context(t("rt_rcs", [128, 16], F32))
        nmz = s_keep.enter_context(t("rt_nmz", [1, 1], F32))
        ccs = s_keep.enter_context(t("rt_cc", [1, 2], F32))

        p.scope(f"rt{call}.A")
        with t("rt_h", [128, 2 * T], F32) as hstg, t("rt_m", [128, 2 * T], BF16) as mstg, \
             t("rt_l", [128, 2 * T], BF16) as lstg, t("rt_r", [128, T], F32) as rstg, \
             t("rt_sq", [128, T], BF16) as rsq, t("rt_b", [128, T], BF16) as bsq:
            p.op("vector", lambda e: e.memset(self.st1[:, :], 0.0))
            p.op("vector", lambda e: e.memset(self.st2[:, :], 0.0))
            H = lambda sl: hstg[:, sl * T:(sl + 1) * T]
            M = lambda sl: mstg[:, sl * T:(sl + 1) * T]
            L = lambda sl: lstg[:, sl * T:(sl + 1) * T]
            cons = {0: [], 1: []}
            dtoks = {}

            def prefetch(c):
                sl = c % 2
                w = list(cons[sl])
                dtoks[c] = (
                    p.dma("sync", "dX", lambda e, _c=c, _sl=sl: e.dma_start(
                        out=H(_sl), in_=self.spill_h[_c * 128:(_c + 1) * 128, :]),
                        waits=sph_w + w),
                    p.dma("sync", "dY", lambda e, _c=c, _sl=sl: e.dma_start(
                        out=M(_sl), in_=self.spill_mu[_c * 128:(_c + 1) * 128, :]),
                        waits=pr_w + w),
                    p.dma("sync", "dZ", lambda e, _c=c, _sl=sl: e.dma_start(
                        out=L(_sl), in_=self.spill_lv[_c * 128:(_c + 1) * 128, :]),
                        waits=pr_w + w))

            cons[0] = cons[1] = [p.w("vector", p.clock("vector")),
                                 p.w("scalar", p.clock("scalar")),
                                 p.w("tensor", p.clock("tensor"))]
            prefetch(0)
            prefetch(1)
            cons = {0: [], 1: []}
            squ = 0
            st1a = 0
            st2a = 0
            for c in range(NF):
                sl = c % 2
                d1, d2, d3 = dtoks[c]
                w1 = [p.dw("dX", d1)]
                if squ:
                    w1.append(p.w("scalar", squ))
                v1 = p.op("vector", lambda e, _c=c, _sl=sl: e.tensor_sub(
                    rstg[:, :], xc(_c, 0, T), H(_sl)), waits=w1)
                sq1 = p.op("scalar", lambda e: e.activation(rsq[:, :], rstg[:, :], AF.Square),
                           waits=[p.w("vector", v1)])
                mm = 0
                for nt in range(4):
                    s = nt * 512
                    mw = [p.w("scalar", sq1)]
                    if st1a:
                        mw.append(p.w("vector", st1a))
                    mm = p.op("tensor", lambda e, _s=s: e.matmul(
                        self.bank(0, 0, 1), self.onesb_c[:, :], rsq[:, _s:_s + 512],
                        start=True, stop=True), waits=mw)
                    st1a = p.op("vector", lambda e, _s=s: e.tensor_tensor(
                        self.st1[:, _s:_s + 512], self.st1[:, _s:_s + 512],
                        self.bank(0, 0, 1), ALU.add), waits=[p.w("tensor", mm)])
                v2 = p.op("vector", lambda e, _sl=sl: e.tensor_sub(
                    rstg[:, :], rstg[:, :], M(_sl)), waits=[p.dw("dY", d2)])
                squ = p.op("scalar", lambda e: e.activation(rsq[:, :], rstg[:, :], AF.Square),
                           waits=[p.w("vector", v2), p.w("tensor", mm)])
                sqe = p.op("scalar", lambda e, _sl=sl: e.activation(
                    M(_sl), L(_sl), AF.Exp, scale=-1.0),
                    waits=[p.dw("dZ", d3), p.w("vector", v2)])
                p.op("vector", lambda e: e.tensor_scalar(
                    rstg[:, :], rsq[:, :], 1.0, None, ALU.add), waits=[p.w("scalar", squ)])
                p.op("vector", lambda e, _sl=sl: e.tensor_mul(
                    rstg[:, :], rstg[:, :], M(_sl)), waits=[p.w("scalar", sqe)])
                v6 = p.op("vector", lambda e, _sl=sl: e.tensor_tensor(
                    bsq[:, :], rstg[:, :], L(_sl), ALU.add))
                cons[sl] = [p.w("vector", v6), p.w("scalar", sqe)]
                if c + 2 < NF:
                    prefetch(c + 2)
                for nt in range(4):
                    s = nt * 512
                    mw = [p.w("vector", v6)]
                    if st2a:
                        mw.append(p.w("vector", st2a))
                    mm2 = p.op("tensor", lambda e, _s=s: e.matmul(
                        self.bank(1, 0, 1), self.onesb_c[:, :], bsq[:, _s:_s + 512],
                        start=True, stop=True), waits=mw)
                    st2a = p.op("vector", lambda e, _s=s: e.tensor_tensor(
                        self.st2[:, _s:_s + 512], self.st2[:, _s:_s + 512],
                        self.bank(1, 0, 1), ALU.add), waits=[p.w("tensor", mm2)])
            p.op("vector", lambda e: e.tensor_scalar(self.st1[:, :], self.st1[:, :], 1.0 / D, None, ALU.mult))
            p.op("vector", lambda e: e.scalar_tensor_tensor(
                self.st1[:, :], self.st2[:, :], 0.5 / D, self.st1[:, :], ALU.mult, ALU.add))
            p.op("vector", lambda e: e.memset(ccs[:, :], 0.0))
            vr = p.op("vector", lambda e: e.reduce_sum(ccs[0:1, 0:1], self.st1[:, :],
                                                       axis=mybir.AxisListType.X))
            dcc = p.dma("sync", "dX", lambda e: e.dma_start(
                out=self.cc_in[call][:, :], in_=ccs[:, :]), waits=[p.w("vector", vr)])
            gcc = p.op("gpsimd", lambda e: e.collective_compute(
                "AllReduce", ALU.add, replica_groups=[list(range(NCORE))],
                ins=[self.cc_in[call][:, :]], outs=[self.cc_out[call][:, :]]),
                waits=[p.dw("dX", dcc)])
            dcb = p.dma("sync", "dX", lambda e: e.dma_start(
                out=ccs[:, :], in_=self.cc_out[call][:, :]), waits=[p.w("gpsimd", gcc)])
            p.op("vector", lambda e: e.tensor_scalar(nmz[:, :], ccs[0:1, 0:1],
                                                     -1.0 / (NCORE * T), None, ALU.mult),
                 waits=[p.dw("dX", dcb)])
            sg = p.op("scalar", lambda e: e.activation(self.st1[:, :], self.st1[:, :], AF.Sigmoid,
                                                       bias=nmz[0:1, 0:1]),
                      waits=[p.w("vector", p.clock("vector"))])
            p.barrier(("tensor", "vector", "scalar"))

        p.scope(f"rt{call}.B")
        import contextlib
        sB = contextlib.ExitStack()
        gbc = sB.enter_context(t("rt_gbc", [128, T], F32))
        dmp = sB.enter_context(t("rt_dmp", [128, T], F32))
        tau = sB.enter_context(t("rt_tau", [128, 1], F32))
        cnt = sB.enter_context(t("rt_cnt", [128, 1], F32))
        cbf = sB.enter_context(t("rt_cbf", [128, 1], BF16))
        sc1 = sB.enter_context(t("rt_sc1", [1, 128], F32))
        lo = sB.enter_context(t("rt_lo", [1, 1], F32))
        hi = sB.enter_context(t("rt_hi", [1, 1], F32))
        cgt = sB.enter_context(t("rt_cgt", [1, 1], F32))
        gcm = sB.enter_context(t("rt_gcm", [128, 16], F32))
        lhb = sB.enter_context(t("rt_lhb", [128, 2], F32))
        bcm = sB.enter_context(t("rt_bcm", [128, 16], F32))
        gtm = sB.enter_context(t("rt_gtm", [128, 16], F32))
        xbf = sB.enter_context(t("rt_xbf", [128, 16], BF16))
        pfx = sB.enter_context(t("rt_pfx", [128, 16], F32))
        tot = sB.enter_context(t("rt_tot", [1, 16], F32))
        tt2 = sB.enter_context(t("rt_tt2", [1, 16], F32))
        rbc = sB.enter_context(t("rt_rbc", [128, 1], F32))
        mrow = sB.enter_context(t("rt_mrow", [128, 16], F32))
        ident32 = sB.enter_context(t("rt_id32", [128, 128], F32))
        ustrict = sB.enter_context(t("rt_ustr", [128, 128], BF16))
        if True:
            di = p.dma("sync", "dX", lambda e: e.dma_start(out=ident32[:, :], in_=self.CT["ident_f32"][:, :]))
            di2 = p.dma("sync", "dX", lambda e: e.dma_start(out=ustrict[:, :], in_=self.CT["ustrict"][:, :]))
            p.op("tensor", lambda e: e.nop(), waits=[di, di2])
            gev = 0
            for nt in range(4):
                s = nt * 512
                mm = p.op("tensor", lambda e, _s=s: e.matmul(
                    self.bank(5), self.ones_r[:, :], self.st1[:, _s:_s + 512], start=True, stop=True),
                    waits=([p.w("vector", gev)] if gev else []))
                gev = p.op("vector", lambda e, _s=s: e.tensor_copy(gbc[:, _s:_s + 512], self.bank(5)),
                           waits=[p.w("tensor", mm)])
            p.op("vector", lambda e: e.memset(lo[:, :], 0.0))
            for rnd in range(4):
                step = 1.0 / (128.0 ** (rnd + 1))
                mm = p.op("tensor", lambda e: e.matmul(
                    self.bank(4, 0, 128, 0, 1), self.ones_r[:, :], lo[:, :], start=True, stop=True),
                    waits=[p.w("vector", p.clock("vector"))])
                p.op("vector", lambda e, _st=step: e.scalar_tensor_tensor(
                    tau[:, :], self.iota1p[:, :], _st, self.bank(4, 0, 128, 0, 1),
                    ALU.mult, ALU.add), waits=[p.w("tensor", mm)])
                p.op("vector", lambda e: e.tensor_scalar(
                    dmp[:, :], gbc[:, :], tau[:, 0:1], None, ALU.is_gt, ALU.add,
                    accum_out=cnt[:, 0:1]))
                vb = p.op("vector", lambda e: e.tensor_scalar(
                    cbf[:, :], cnt[:, :], 511.5, self.iota1p[:, 0:1], ALU.is_gt, ALU.mult))
                slot, tw = self.tslot()
                tt = p.op("tensor", lambda e, _sl=slot: e.transpose(
                    _sl[0:1, 0:128], cbf[:, 0:1], self.ident[:, :]),
                    waits=[p.w("vector", vb)] + tw)
                te = p.op("vector", lambda e, _sl=slot: e.tensor_copy(sc1[0:1, :], _sl[0:1, 0:128]),
                          waits=[p.w("tensor", tt)])
                self.tslot_done(te)
                p.op("vector", lambda e: e.reduce_max(hi[:, :], sc1[0:1, :],
                                                      axis=mybir.AxisListType.X))
                p.op("vector", lambda e, _st=step: e.scalar_tensor_tensor(
                    lo[:, :], hi[:, :], _st, lo[:, :], ALU.mult, ALU.add))
            fstep = 1.0 / (128.0 ** 4)
            p.op("vector", lambda e: e.tensor_scalar(hi[:, :], lo[:, :], fstep, None, ALU.add))
            vc = p.op("vector", lambda e: e.tensor_scalar(
                dmp[0:1, :], self.st1[:, :], hi[0:1, 0:1], None, ALU.is_gt, ALU.add,
                accum_out=cgt[0:1, 0:1]))
            p.op("vector", lambda e: e.tensor_scalar(cgt[:, :], cgt[:, :], -1.0, 512.0,
                                                     ALU.mult, ALU.add))
            mm = p.op("tensor", lambda e: e.matmul(
                self.bank(4, 0, 128, 0, 1), self.ones_r[:, :], lo[:, :], start=True, stop=True),
                waits=[p.w("vector", p.clock("vector"))])
            p.op("vector", lambda e: e.tensor_copy(lhb[:, 0:1], self.bank(4, 0, 128, 0, 1)),
                 waits=[p.w("tensor", mm)])
            mm = p.op("tensor", lambda e: e.matmul(
                self.bank(4, 0, 128, 0, 1), self.ones_r[:, :], hi[:, :], start=True, stop=True),
                waits=[p.w("vector", p.clock("vector"))])
            p.op("vector", lambda e: e.tensor_copy(lhb[:, 1:2], self.bank(4, 0, 128, 0, 1)),
                 waits=[p.w("tensor", mm)])
            tb = {4: 0, 5: 0}
            for tc in range(NTT):
                bk = (4, 5)[tc % 2]
                w = [p.w("vector", p.clock("vector"))]
                tt = p.op("tensor", lambda e, _tc=tc, _bk=bk: e.transpose(
                    self.bank(_bk, 0, 128, 0, 128), gbc[:, _tc * 128:(_tc + 1) * 128],
                    ident32[:, :]), waits=w)
                tb[bk] = p.op("vector", lambda e, _tc=tc, _bk=bk: e.tensor_copy(
                    gcm[:, _tc:_tc + 1], self.bank(_bk, 0, 128, 0, 1)),
                    waits=[p.w("tensor", tt)])
            p.op("vector", lambda e: e.tensor_scalar(gtm[:, :], gcm[:, :], lhb[:, 1:2], None, ALU.is_gt))
            p.op("vector", lambda e: e.tensor_scalar(bcm[:, :], gcm[:, :], lhb[:, 0:1], None, ALU.is_gt))
            p.op("vector", lambda e: e.tensor_sub(bcm[:, :], bcm[:, :], gtm[:, :]))

            def prefix_cm(srcf, dstf):
                vv = p.op("vector", lambda e: e.tensor_copy(xbf[:, :], srcf[:, :]))
                mm1 = p.op("tensor", lambda e: e.matmul(
                    self.bank(4, 0, 128, 0, 16), ustrict[:, :], xbf[:, :],
                    start=True, stop=True), waits=[p.w("vector", vv)])
                mm2 = p.op("tensor", lambda e: e.matmul(
                    self.bank(5, 0, 1, 0, 16), self.onesb_c[:, :], xbf[:, :],
                    start=True, stop=True))
                p.op("vector", lambda e: e.tensor_copy(dstf[:, :], self.bank(4, 0, 128, 0, 16)),
                     waits=[p.w("tensor", mm1)])
                p.op("vector", lambda e: e.tensor_copy(tot[:, :], self.bank(5, 0, 1, 0, 16)),
                     waits=[p.w("tensor", mm2)])
                p.op("vector", lambda e: e.tensor_copy(tt2[:, :], tot[:, :]))
                for sh in (1, 2, 4, 8):
                    p.op("vector", lambda e, _s=sh: e.tensor_tensor(
                        tt2[0:1, _s:16], tt2[0:1, _s:16], tt2[0:1, 0:16 - _s], ALU.add))
                p.op("vector", lambda e: e.tensor_sub(tt2[:, :], tt2[:, :], tot[:, :]))
                mm3 = p.op("tensor", lambda e: e.matmul(
                    self.bank(5, 0, 128, 0, 16), self.ones_r[:, :], tt2[:, :],
                    start=True, stop=True), waits=[p.w("vector", p.clock("vector"))])
                p.op("vector", lambda e: e.tensor_tensor(
                    dstf[:, :], dstf[:, :], self.bank(5, 0, 128, 0, 16), ALU.add),
                    waits=[p.w("tensor", mm3)])

            prefix_cm(bcm, pfx)
            mm = p.op("tensor", lambda e: e.matmul(
                self.bank(4, 0, 128, 0, 1), self.ones_r[:, :], cgt[:, :], start=True, stop=True),
                waits=[p.w("vector", p.clock("vector"))])
            p.op("vector", lambda e: e.tensor_copy(rbc[:, :], self.bank(4, 0, 128, 0, 1)),
                 waits=[p.w("tensor", mm)])
            p.op("vector", lambda e: e.tensor_scalar(mrow[:, :], pfx[:, :], rbc[:, 0:1], None, ALU.is_lt))
            p.op("vector", lambda e: e.tensor_mul(mrow[:, :], mrow[:, :], bcm[:, :]))
            p.op("vector", lambda e: e.tensor_tensor(mrow[:, :], mrow[:, :], gtm[:, :], ALU.add))
            p.op("vector", lambda e: e.tensor_copy(mcs[:, :], mrow[:, :]))
            prefix_cm(mrow, rcs)
            p.op("vector", lambda e: e.tensor_copy(gcs[:, :], gcm[:, :]))
            p.barrier(("tensor", "vector", "scalar", "sync"))
        sB.close()

        gcs = s_keep.enter_context(t("rt_gcs", [128, 16], BF16))
        sel = s_keep.enter_context(t("rt_sel", [128, NF * 512], F32))
        xd2 = s_keep.enter_context(t("rt_xd2", [128, NF * 512], F32))
        Pm = s_keep.enter_context(t("rt_Pm", [128, 4 * T], BF16))
        cos2 = s_keep.enter_context(t("rt_cos2", [128, 512], BF16))
        sin2 = s_keep.enter_context(t("rt_sin2", [128, 512], BF16))
        gse = s_keep.enter_context(t("rt_gse", [1, 512], F32))

        p.scope(f"rt{call}.C")
        with t("rt_PT", [128, NTT * 512], BF16) as PT, t("rt_iota", [128, 512], F32) as iota512, \
             t("rt_hib", [128, T], BF16) as hib, t("rt_lob", [128, T], BF16) as lob, \
             t("rt_htm", [128, NTT * 128], BF16) as htm, t("rt_ltm", [128, NTT * 128], BF16) as ltm, \
             t("rt_exb", [128, NTT * 65], BF16) as exb, t("rt_cse", [64, 512], BF16) as cse, \
             t("rt_ctm", [128, NTT * 32], BF16) as ctm, t("rt_stm", [128, NTT * 32], BF16) as stm:
            gio = p.op("gpsimd", lambda e: e.iota(iota512[:, :], [[1, 512]], base=0,
                                                  channel_multiplier=0,
                                                  allow_small_or_imprecise_dtypes=True),
                   waits=[p.w("vector", p.clock("vector")), p.w("tensor", p.clock("tensor")),
                          p.w("scalar", p.clock("scalar"))])
            for tc in range(NTT):
                p.op("vector", lambda e, _tc=tc: e.tensor_scalar(
                    PT[:, _tc * 512:(_tc + 1) * 512], iota512[:, :],
                    rcs[:, _tc:_tc + 1], mcs[:, _tc:_tc + 1], ALU.is_equal, ALU.mult),
                    waits=[p.w("gpsimd", gio)] if tc == 0 else ())
            pt_cl = p.clock("vector")
            gsel_ev = {2: 0, 3: 0}
            for fc in range(NF):
                bk = (2, 3)[fc % 2]
                p.op("vector", lambda e, _c=fc: e.tensor_copy(hib[:, :], xc(_c, 0, T)))
                p.op("vector", lambda e, _c=fc: e.tensor_tensor(lob[:, :], xc(_c, 0, T),
                                                                hib[:, :], ALU.subtract))
                tev = 0
                for tc in range(NTT):
                    for srcb, dstt in ((hib, htm), (lob, ltm)):
                        slot, tw = self.tslot()
                        tt = p.op("tensor", lambda e, _tc=tc, _sl=slot, _src=srcb: e.transpose(
                            _sl[:, :], _src[:, _tc * 128:(_tc + 1) * 128], self.ident[:, :]),
                            waits=[p.w("vector", p.clock("vector"))] + tw)
                        tev = p.op("vector", lambda e, _tc=tc, _sl=slot, _d=dstt: e.tensor_copy(
                            _d[:, _tc * 128:(_tc + 1) * 128], _sl[:, :]),
                            waits=[p.w("tensor", tt)])
                w0 = [p.w("vector", tev)]
                if gsel_ev[bk]:
                    w0.append(p.w("vector", gsel_ev[bk]))
                nmm = 0
                for tc in range(NTT):
                    for srcb in (htm, ltm):
                        mm = p.op("tensor", lambda e, _bk=bk, _tc=tc, _s=srcb,
                                  _f=(nmm == 0), _l=(nmm == 2 * NTT - 1): e.matmul(
                            self.bank(_bk), _s[:, _tc * 128:(_tc + 1) * 128],
                            PT[:, _tc * 512:(_tc + 1) * 512], start=_f, stop=_l),
                            waits=w0 if nmm == 0 else ())
                        nmm += 1
                gsel_ev[bk] = p.op("vector", lambda e, _bk=bk, _c=fc: e.tensor_copy(
                    sel[:, _c * 512:(_c + 1) * 512], self.bank(_bk)), waits=[p.w("tensor", mm)])
            d8 = p.dma("sync", "dX", lambda e: e.dma_start(
                out=ctm[:, :], in_=self.CT["cos_tm"][:, :]))
            d9 = p.dma("sync", "dX", lambda e: e.dma_start(
                out=stm[:, :], in_=self.CT["sin_tm"][:, :]))
            p.op("vector", lambda e: e.nop(),
                 waits=[p.dw("dX", d9), p.dw("dX", d8)])
            for tc in range(NTT):
                p.op("vector", lambda e, _tc=tc: e.tensor_copy(
                    exb[:, _tc * 65:_tc * 65 + 32], ctm[:, _tc * 32:(_tc + 1) * 32]))
                p.op("vector", lambda e, _tc=tc: e.tensor_copy(
                    exb[:, _tc * 65 + 32:_tc * 65 + 64], stm[:, _tc * 32:(_tc + 1) * 32]))
                p.op("vector", lambda e, _tc=tc: e.tensor_copy(
                    exb[:, _tc * 65 + 64:_tc * 65 + 65], gcs[:, _tc:_tc + 1]))
            p.op("tensor", lambda e: e.nop(), waits=[p.w("vector", p.clock("vector"))])
            for tc in range(NTT):
                mm = p.op("tensor", lambda e, _tc=tc, _f=(tc == 0), _l=(tc == NTT - 1): e.matmul(
                    self.bank(4, 0, 65), exb[:, _tc * 65:(_tc + 1) * 65],
                    PT[:, _tc * 512:(_tc + 1) * 512], start=_f, stop=_l))
            p.op("vector", lambda e: e.tensor_copy(cse[0:32, :], self.bank(4, 0, 32)),
                 waits=[p.w("tensor", mm)])
            p.op("vector", lambda e: e.tensor_copy(cse[32:64, :], self.bank(4, 32, 64)))
            exs_ev = p.op("vector", lambda e: e.tensor_copy(gse[:, :], self.bank(4, 64, 65)))
            mm1 = p.op("tensor", lambda e: e.matmul(
                self.bank(5), self.repm_c[:, :], cse[0:64, :], start=True, stop=True),
                waits=[p.w("vector", exs_ev)])
            c2e = p.op("vector", lambda e: e.tensor_copy(cos2[:, :], self.bank(5)),
                       waits=[p.w("tensor", mm1)])
            mm2 = p.op("tensor", lambda e: e.matmul(
                self.bank(5), self.repm_s[:, :], cse[0:64, :], start=True, stop=True),
                waits=[p.w("vector", c2e)])
            p.op("vector", lambda e: e.tensor_copy(sin2[:, :], self.bank(5)),
                 waits=[p.w("tensor", mm2)])
            for c in range(NF):
                p.op("vector", lambda e, _c=c: e.tensor_copy(
                    xd2[:, _c * 512:(_c + 1) * 512], sel[:, _c * 512:(_c + 1) * 512]))
            pm_cl = 0
            for tc in range(NTT):
                for sc in range(4):
                    slot, tw = self.tslot()
                    tt = p.op("tensor", lambda e, _tc=tc, _sc=sc, _sl=slot: e.transpose(
                        _sl[:, :], PT[:, _tc * 512 + _sc * 128:_tc * 512 + (_sc + 1) * 128],
                        self.ident[:, :]), waits=[p.w("vector", pt_cl)] + tw)
                    pm_cl = p.op("vector", lambda e, _tc=tc, _sc=sc, _sl=slot: e.tensor_copy(
                        Pm[:, _sc * T + _tc * 128:_sc * T + (_tc + 1) * 128], _sl[:, :]),
                        waits=[p.w("tensor", tt)])
                    self.tslot_done(pm_cl)
            p.barrier(("tensor", "vector", "scalar", "sync", "gpsimd"))

        xc2 = lambda c, t0, t1: xd2[:, c * 512 + t0: c * 512 + t1]
        self.decoder(xc2, 512, "fWq2", "fWk2", "fWv2", "fWo2", 16, 24, "fWg2", "fWu2", "fWd2",
                     lambda s: cos2[:, 0:512], lambda s: sin2[:, 0:512], f"d2_{call}")

        p.scope(f"rt{call}.blend")
        with t("rt_dstg", [128, 512], F32) as dstg, t("rt_dbf", [128, 512], BF16) as dbf, \
             t("rt_dtm", [128, 4 * 1024], BF16) as dtm:
            mmg = p.op("tensor", lambda e: e.matmul(
                self.bank(5), self.ones_r[:, :], gse[:, :], start=True, stop=True),
                waits=[p.w("vector", p.clock("vector"))])
            gw = p.w("tensor", mmg)
            dt_cl = 0
            for fc in range(NF):
                p.op("vector", lambda e, _c=fc: e.tensor_tensor(
                    dstg[:, :], xd2[:, _c * 512:(_c + 1) * 512], sel[:, _c * 512:(_c + 1) * 512],
                    ALU.subtract), waits=[gw])
                p.op("vector", lambda e: e.tensor_mul(dbf[:, :], dstg[:, :], self.bank(5)))
                for sc in range(4):
                    slot, tw = self.tslot()
                    tt = p.op("tensor", lambda e, _sc=sc, _sl=slot: e.transpose(
                        _sl[:, :], dbf[:, _sc * 128:(_sc + 1) * 128], self.ident[:, :]),
                        waits=[p.w("vector", p.clock("vector"))] + tw)
                    dt_cl = p.op("vector", lambda e, _sc=sc, _c=fc, _sl=slot: e.tensor_copy(
                        dtm[:, _sc * 1024 + _c * 128:_sc * 1024 + (_c + 1) * 128], _sl[:, :]),
                        waits=[p.w("tensor", tt)])
                    self.tslot_done(dt_cl)
            sc_ev = {2: 0, 3: 0}
            for mc in range(NF):
                bk = (2, 3)[mc % 2]
                for nt in range(4):
                    s = nt * 512
                    w0 = [p.w("vector", dt_cl)]
                    if sc_ev[bk]:
                        w0.append(p.w("vector", sc_ev[bk]))
                    for sc in range(4):
                        mm = p.op("tensor", lambda e, _bk=bk, _sc=sc, _mc=mc, _s=s,
                                  _f=(sc == 0), _l=(sc == 3): e.matmul(
                            self.bank(_bk),
                            dtm[:, _sc * 1024 + _mc * 128:_sc * 1024 + (_mc + 1) * 128],
                            Pm[:, _sc * T + _s:_sc * T + _s + 512], start=_f, stop=_l),
                            waits=w0 if sc == 0 else ())
                    sc_ev[bk] = p.op("vector", lambda e, _bk=bk, _mc=mc, _s=s: e.tensor_tensor(
                        xc(_mc, _s, _s + 512), xc(_mc, _s, _s + 512), self.bank(_bk), ALU.add),
                        waits=[p.w("tensor", mm)])
            p.barrier()
        s_keep.close()

    def store_out(self, buf):
        self.p.scope("store")
        p = self.p
        cl = {e: p.clock(e) for e in ("tensor", "vector", "scalar")}
        for c in range(NF):
            p.dma("sync", "dX", lambda e, _c=c: e.dma_start(
                out=self.outT[_c * 128:(_c + 1) * 128, :], in_=self.fm(buf, _c)),
                waits=[p.w(en, cl[en]) for en in cl])
        p.op("sync", lambda e: e.nop(), waits=p.qwait("dX"))

    def finish(self):
        with self.nc.Block() as block:
            self.p.emit(block)
        self.stk.close()
        return self.nc


def build(tap=""):
    mk = MK()
    mk.stage_init()
    if tap == "x0":
        mk.store_out(mk.A)
        return mk.finish()
    mk.attn_a()
    if tap == "x1":
        mk.store_out(mk.A)
        return mk.finish()
    if tap in ("xp", "xq", "xa", "xk"):
        mk.ffn(1, stop_after={"xp": "dec1", "xq": "prior", "xa": "dattn",
                              "xk": "dqk"}[tap])
        mk.store_out(mk.A)
        return mk.finish()
    mk.ffn(1)
    if tap == "x2":
        mk.store_out(mk.A)
        return mk.finish()
    mk.attn_b()
    if tap == "x3":
        mk.store_out(mk.A)
        return mk.finish()
    mk.ffn(2)
    mk.store_out(mk.A)
    return mk.finish()


_CACHE = {}


def kernel(**inputs):
    tap = TAP
    if _CACHE.get("tap") != tap:
        _CACHE["nc"] = build(tap)
        _CACHE["tap"] = tap
    nc = _CACHE["nc"]
    if "w" not in _CACHE:
        _CACHE["w"] = host_prepare(inputs)
    Wh = _CACHE["w"]
    x = np.asarray(inputs["L0_attn_a_x"])
    in_maps = []
    for i in range(NCORE):
        m = dict(Wh)
        m["xT"] = np.ascontiguousarray(x[i].T)
        in_maps.append(m)
    res = run_bass_kernel_spmd(nc, in_maps, core_ids=list(range(NCORE)))
    out = np.stack([np.ascontiguousarray(r["outT"].T) for r in res.results])
    return out.astype(np.float32)
